# version 02 (final)
# baseline (speedup 1.0000x reference)
import os
import numpy as np
import ml_dtypes
from contextlib import ExitStack
import concourse.bass as bass
import concourse.mybir as mybir
from concourse.bass_utils import run_bass_kernel_spmd

F32 = mybir.dt.float32
BF16 = mybir.dt.bfloat16
AF = mybir.ActivationFunctionType
ALU = mybir.AluOpType
AX = mybir.AxisListType
BFNP = ml_dtypes.bfloat16

NCORES = 8
D = 2048
T = 2048
NFT = D // 128
TT = 512
NTT = T // TT
NEG = -1.0e30
LN_EPS = 1e-5

ENGS = ["sync", "gpsimd", "tensor", "scalar", "vector"]

SUB_FROM = int(os.environ.get("MK_SUB_FROM", "0"))
SUB_TO = int(os.environ.get("MK_SUB_TO", "8"))
FUSED = os.environ.get("MK_FUSED", "1") == "1"
DEBUG_OUT = None



def pack_w(W):
    W = np.asarray(W, dtype=np.float32)
    din, dout = W.shape
    assert din % 128 == 0 and dout % 128 == 0, (din, dout)
    fic, foc = din // 128, dout // 128
    return np.ascontiguousarray(
        W.reshape(fic, 128, foc, 128).transpose(1, 2, 0, 3))


def pack_w_rhs(W):
    W = np.asarray(W, dtype=np.float32)
    din, dout = W.shape
    assert din % 128 == 0
    fic = din // 128
    return np.ascontiguousarray(W.reshape(fic, 128, dout).transpose(1, 0, 2))


def row(v):
    return np.ascontiguousarray(np.asarray(v, dtype=np.float32).reshape(1, -1))


def colv(v):
    return np.ascontiguousarray(np.asarray(v, dtype=np.float32).reshape(-1, 1))


def pcol(v):
    return np.ascontiguousarray(
        np.asarray(v, dtype=np.float32).reshape(-1, 128).T)



class Sem:
    def __init__(self, h):
        self.h = h
        self.v = 0

    def reserve(self, n=1):
        self.v += n
        return self.v


class Phase:

    def __init__(self, K, name):
        self.K = K
        self.name = name
        self.q = {e: [] for e in ENGS}

    def on(self, eng):
        def reg(fn):
            self.q[eng].append(fn)
            return fn
        return reg

    def emit(self):
        with self.K.nc.Block() as blk:
            for e in ENGS:
                fns = self.q[e]
                if fns:
                    getattr(blk, e)(lambda eng, fns=fns: [f(eng) for f in fns])


class Ctx:
    def __init__(self, tag):
        self.nc = bass.Bass("TRN2", target_bir_lowering=False)
        self.es = ExitStack()
        self.in_maps = [dict() for _ in range(NCORES)]
        self.tag = tag
        self._uid = 0
        self._inputs = {}
        self.sem = {}
        for name in ["ld", "wd", "st", "mm", "ev", "a1", "a2", "a3", "cc", "xr",
                     "vbar", "gbar", "sbar", "tbar",
                     "rs0", "rs1", "rs2", "rs3", "ws0", "ws1", "ws2", "ws3"]:
            self.sem[name] = Sem(self.es.enter_context(self.nc.semaphore(name)))
        self.barv = {"vector": 0, "gpsimd": 0, "scalar": 0, "tensor": 0}
        self._barmap = {"vector": "vbar", "gpsimd": "gbar",
                        "scalar": "sbar", "tensor": "tbar"}
        self.C = None
        self.dbg = []

    def dsem(self, base):
        name = self.uname(base)
        s = Sem(self.es.enter_context(self.nc.semaphore(name)))
        self.sem[name] = s
        return s

    def dbg_dump(self, name, ap, shape, dtype):
        if os.environ.get("MK_DEBUG"):
            self.dbg.append((name, ap, shape, dtype))

    def uname(self, base):
        self._uid += 1
        return f"{base}_{self._uid}"

    def input(self, name, arrs, dtype=F32):
        if name in self._inputs:
            return self._inputs[name]
        if isinstance(arrs, np.ndarray):
            arrs = [arrs] * NCORES
        ap = self.nc.dram_tensor(name, list(arrs[0].shape), dtype,
                                 kind="ExternalInput").ap()
        for c in range(NCORES):
            a = np.ascontiguousarray(np.asarray(arrs[c], dtype=np.float32))
            self.in_maps[c][name] = a
        self._inputs[name] = ap
        return ap

    def scratch(self, name, shape, dtype):
        return self.nc.dram_tensor(self.uname(name), list(shape), dtype).ap()

    def output(self, name, shape, dtype=F32):
        return self.nc.dram_tensor(name, list(shape), dtype,
                                   kind="ExternalOutput").ap()

    def bar(self, eng_name, eng, ins):
        self.barv[eng_name] += 1
        s = self.sem[self._barmap[eng_name]]
        ins.then_inc(s.h, 1)
        eng.wait_ge(s.h, self.barv[eng_name])
        return ins

    def sb(self, base, shape, dtype):
        return self.es.enter_context(
            self.nc.sbuf_tensor(self.uname(base), list(shape), dtype))

    def ps(self, base, shape, dtype=F32):
        return self.es.enter_context(
            self.nc.psum_tensor(self.uname(base), list(shape), dtype))


def ssb(K, scope, base, shape, dtype):
    return scope.enter_context(
        K.nc.sbuf_tensor(K.uname(base), list(shape), dtype))


def sps(K, scope, base, shape, dtype=F32):
    return scope.enter_context(
        K.nc.psum_tensor(K.uname(base), list(shape), dtype))



class Consts:
    pass


def const_host_arrays():
    host = {}
    host["c_ident"] = np.eye(128, dtype=np.float32)
    host["c_ones_row"] = np.ones((1, 512), dtype=np.float32)
    host["c_ones_col"] = np.ones((128, 1), dtype=np.float32)
    host["c_allones"] = np.ones((128, 128), dtype=np.float32)
    dm = np.zeros((4, 128, 512), dtype=np.float32)
    for j in range(4):
        k = np.arange(128)[:, None] + 128 * j
        q = np.arange(512)[None, :]
        dm[j] = np.where(k <= q, 0.0, NEG)
    host["c_diagmask"] = dm.reshape(4 * 128, 512)
    gm = np.zeros((2, 128, 256), dtype=np.float32)
    for sh in range(2):
        s = np.arange(128)[:, None] + 128 * sh
        l = np.arange(256)[None, :]
        gm[sh] = (s <= l).astype(np.float32)
    host["c_gmask"] = gm.reshape(2 * 128, 256)
    sel = np.zeros((64, 64, 128), dtype=np.float32)
    for h in range(64):
        sel[h, h, :] = 1.0
    host["c_sel"] = sel.reshape(64 * 64, 128)
    return host


def emit_consts(K):
    host = const_host_arrays()
    aps = {k: K.input(k, v) for k, v in host.items()}
    C = Consts()
    C.identb = K.sb("identb", [128, 128], BF16)
    C.identf = K.sb("identf", [128, 128], F32)
    C.ones_row = K.sb("ones_row", [1, 512], BF16)
    C.ones_col = K.sb("ones_col", [128, 1], BF16)
    C.ones_colf = K.sb("ones_colf", [128, 1], F32)
    C.allones = K.sb("allones", [128, 128], BF16)
    ld = K.sem["ld"]
    ph = Phase(K, "consts")
    tgt = ld.reserve(16 * 6)

    @ph.on("gpsimd")
    def _(g):
        g.dma_start(out=C.identb[:, :], in_=aps["c_ident"][:, :]).then_inc(ld.h, 16)
        g.dma_start(out=C.identf[:, :], in_=aps["c_ident"][:, :]).then_inc(ld.h, 16)
        g.dma_start(out=C.ones_row[:, :], in_=aps["c_ones_row"][:, :]).then_inc(ld.h, 16)
        g.dma_start(out=C.ones_col[:, :], in_=aps["c_ones_col"][:, :]).then_inc(ld.h, 16)
        g.dma_start(out=C.ones_colf[:, :], in_=aps["c_ones_col"][:, :]).then_inc(ld.h, 16)
        g.dma_start(out=C.allones[:, :], in_=aps["c_allones"][:, :]).then_inc(ld.h, 16)
        g.wait_ge(ld.h, tgt)

    ph.emit()
    return C



class LNOut:
    pass


def emit_ln_xn(K, scope, x_res, *, mhsa_gb=None):
    L = LNOut()
    L.xn = ssb(K, scope, "xn", [128, NFT * T], BF16)
    lns = ExitStack()
    L.m_row = ssb(K, lns, "m_row", [1, T], F32)
    L.s_row = ssb(K, lns, "s_row", [1, T], F32)
    L.v_row = ssb(K, lns, "v_row", [1, T], F32)
    L.m_bf = ssb(K, lns, "m_bf", [1, T], BF16)
    L.s_bf = ssb(K, lns, "s_bf", [1, T], BF16)
    L.m_b = ssb(K, lns, "m_b", [128, T], BF16)
    L.s_b = ssb(K, lns, "s_b", [128, T], BF16)
    r1 = ssb(K, lns, "r1_row", [1, T], F32)
    t_row = ssb(K, lns, "t_row", [1, T], F32)
    sq = ssb(K, lns, "sq", [128, 2 * TT], BF16)
    scr = ssb(K, lns, "scr", [128, T], BF16)
    ps_sum = sps(K, lns, "ps_sum", [128, TT])
    ps_sq = sps(K, lns, "ps_sq", [128, TT])
    ps_b = sps(K, lns, "ps_b", [128, TT])

    gcol = bcol = None
    if mhsa_gb is not None:
        gcol = ssb(K, lns, "gcol", [128, NFT], F32)
        bcol = ssb(K, lns, "bcol", [128, NFT], F32)

    ld, mm, ev, a1, a2 = (K.sem[n] for n in ["ld", "mm", "ev", "a1", "a2"])
    ld0, mm0, a10, a20, ev0 = ld.v, mm.v, a1.v, a2.v, ev.v
    ph = Phase(K, "ln_xn")

    n_ld = NFT + (2 if mhsa_gb is not None else 0)
    ld.reserve(16 * n_ld)

    @ph.on("gpsimd")
    def _(g):
        for ft in range(NFT):
            g.dma_start(out=L.xn[:, ft * T:(ft + 1) * T],
                        in_=x_res[ft * 128:(ft + 1) * 128, :]).then_inc(ld.h, 16)
        if mhsa_gb is not None:
            g.dma_start(out=gcol[:, :], in_=mhsa_gb[0][:, :]).then_inc(ld.h, 16)
            g.dma_start(out=bcol[:, :], in_=mhsa_gb[1][:, :]).then_inc(ld.h, 16)

    a1.reserve(NTT * NFT)

    @ph.on("scalar")
    def _(s):
        for tt in range(NTT):
            for ft in range(NFT):
                g = tt * NFT + ft
                s.wait_ge(ld.h, ld0 + 16 * (ft + 1))
                if g >= 2:
                    s.wait_ge(mm.h, mm0 + 2 * (g - 1))
                s.activation(out=sq[:, (g % 2) * TT:(g % 2 + 1) * TT],
                             in_=L.xn[:, ft * T + tt * TT: ft * T + (tt + 1) * TT],
                             func=AF.Square).then_inc(a1.h, 1)

    mm.reserve(2 * NTT * NFT)

    @ph.on("tensor")
    def _(t):
        for tt in range(NTT):
            if tt >= 1:
                t.wait_ge(a2.h, a20 + tt)
            for ft in range(NFT):
                g = tt * NFT + ft
                t.wait_ge(a1.h, a10 + g + 1)
                xs = L.xn[:, ft * T + tt * TT: ft * T + (tt + 1) * TT]
                sqs = sq[:, (g % 2) * TT:(g % 2 + 1) * TT]
                t.matmul(ps_sum[0:1, :], K.C.ones_col[:, 0:1], xs,
                         start=(ft == 0), stop=(ft == NFT - 1))
                t.matmul(ps_sq[0:1, :], K.C.ones_col[:, 0:1], sqs,
                         start=(ft == 0), stop=(ft == NFT - 1)).then_inc(mm.h, 2)

    n_extra_row_steps = 1
    a2.reserve(NTT + 1)

    @ph.on("vector")
    def _(v):
        for tt in range(NTT):
            sl = slice(tt * TT, (tt + 1) * TT)
            v.wait_ge(mm.h, mm0 + 2 * NFT * (tt + 1))
            v.tensor_scalar_mul(L.m_row[0:1, sl], ps_sum[0:1, :], 1.0 / D)
            v.tensor_scalar_mul(L.v_row[0:1, sl], ps_sq[0:1, :],
                                1.0 / D).then_inc(a2.h, 1)
        v.tensor_tensor(out=t_row[0:1, :], in0=L.m_row[0:1, :],
                        in1=L.m_row[0:1, :], op=ALU.mult)
        v.tensor_tensor(out=L.v_row[0:1, :], in0=L.v_row[0:1, :],
                        in1=t_row[0:1, :], op=ALU.subtract)
        v.tensor_scalar_add(t_row[0:1, :], L.v_row[0:1, :], LN_EPS)
        v.reciprocal(r1[0:1, :], t_row[0:1, :]).then_inc(a2.h, 1)

    a1_sq_done = a10 + NTT * NFT
    a2_r1 = a20 + NTT + 1
    a1_s = a1.reserve(1)

    @ph.on("scalar")
    def _(s):
        s.wait_ge(a2.h, a2_r1)
        s.activation(out=L.s_row[0:1, :], in_=r1[0:1, :],
                     func=AF.Sqrt).then_inc(a1.h, 1)

    rows_ready_sem = a1
    rows_ready_val = a1_s

    if mhsa_gb is not None:
        a2_q = a2.reserve(1)

        @ph.on("vector")
        def _(v):
            v.wait_ge(a1.h, a1_s)
            v.tensor_tensor(out=t_row[0:1, :], in0=L.v_row[0:1, :],
                            in1=r1[0:1, :], op=ALU.mult)
            v.tensor_scalar_add(t_row[0:1, :], t_row[0:1, :], LN_EPS)
            v.reciprocal(t_row[0:1, :], t_row[0:1, :]).then_inc(a2.h, 1)

        a1_s2 = a1.reserve(1)

        @ph.on("scalar")
        def _(s):
            s.wait_ge(a2.h, a2_q)
            s.activation(out=t_row[0:1, :], in_=t_row[0:1, :],
                         func=AF.Sqrt).then_inc(a1.h, 1)

        a2_scomb = a2.reserve(1)

        @ph.on("vector")
        def _(v):
            v.wait_ge(a1.h, a1_s2)
            v.tensor_tensor(out=L.s_row[0:1, :], in0=L.s_row[0:1, :],
                            in1=t_row[0:1, :], op=ALU.mult).then_inc(a2.h, 1)
        rows_ready_sem = a2
        rows_ready_val = a2_scomb

    a2_rows_bf = a2.reserve(1)
    rr_sem, rr_val = rows_ready_sem, rows_ready_val

    @ph.on("vector")
    def _(v):
        v.wait_ge(rr_sem.h, rr_val)
        v.tensor_copy(L.m_bf[0:1, :], L.m_row[0:1, :])
        v.tensor_copy(L.s_bf[0:1, :], L.s_row[0:1, :]).then_inc(a2.h, 1)

    mm_b0 = mm.v
    mm.reserve(2 * NTT)
    ev_b0 = ev.v
    ev.reserve(2 * NTT)

    @ph.on("tensor")
    def _(t):
        t.wait_ge(a2.h, a2_rows_bf)
        for i, src in enumerate([L.m_bf, L.s_bf]):
            for tt in range(NTT):
                g = i * NTT + tt
                if g >= 1:
                    t.wait_ge(ev.h, ev_b0 + g)
                t.matmul(ps_b[:, :], K.C.ones_row[0:1, 0:128],
                         src[0:1, tt * TT:(tt + 1) * TT],
                         start=True, stop=True).then_inc(mm.h, 1)

    @ph.on("scalar")
    def _(s):
        for i, dst in enumerate([L.m_b, L.s_b]):
            for tt in range(NTT):
                g = i * NTT + tt
                s.wait_ge(mm.h, mm_b0 + g + 1)
                s.activation(out=dst[:, tt * TT:(tt + 1) * TT], in_=ps_b[:, :],
                             func=AF.Copy).then_inc(ev.h, 1)
    ev_bcast_done = ev.v

    a2.reserve(NFT)
    L.xn_done = a2.v

    @ph.on("vector")
    def _(v):
        v.wait_ge(ev.h, ev_bcast_done)
        for ft in range(NFT):
            xs = L.xn[:, ft * T:(ft + 1) * T]
            v.tensor_tensor(out=scr[:, :], in0=xs, in1=L.m_b[:, :],
                            op=ALU.subtract)
            if mhsa_gb is None:
                v.tensor_tensor(out=xs, in0=scr[:, :], in1=L.s_b[:, :],
                                op=ALU.mult).then_inc(a2.h, 1)
            else:
                v.scalar_tensor_tensor(out=scr[:, :], in0=scr[:, :],
                                       scalar=gcol[:, ft:ft + 1],
                                       in1=L.s_b[:, :],
                                       op0=ALU.mult, op1=ALU.mult)
                v.tensor_scalar_add(xs, scr[:, :],
                                    bcol[:, ft:ft + 1]).then_inc(a2.h, 1)

    ph.emit()
    lns.close()
    return L



def emit_proj(K, scope, name, *, w_dram, fic, foc, rhs_slice, ntt, tt_cols,
              evac, rank1=None, wait_sems=None, ph=None, emit_now=True):
    wd, mm, ev = K.sem["wd"], K.sem["mm"], K.sem["ev"]
    own_ph = ph is None
    if own_ph:
        ph = Phase(K, name)
    wb_stride = fic * 128
    wbuf = ssb(K, scope, name + "_w", [128, 2 * wb_stride], BF16)
    pss = [sps(K, scope, name + "_ps", [128, tt_cols]) for _ in range(2)]
    mm0, ev0 = mm.v, ev.v
    wd0 = wd.v
    n_r1 = len(rank1 or [])
    mm_per_group = fic + n_r1
    wd.reserve(16 * foc)
    mm.reserve(mm_per_group * foc * ntt)
    mm_vals = {}
    ev_vals = {}
    for f in range(foc):
        for tt in range(ntt):
            g = f * ntt + tt
            mm_vals[(f, tt)] = mm0 + (g + 1) * mm_per_group
            ev_vals[(f, tt)] = ev.reserve(1)

    @ph.on("gpsimd")
    def _(g):
        for f in range(foc):
            if f >= 2:
                g.wait_ge(ev.h, ev0 + (f - 1) * ntt)
            g.dma_start(out=wbuf[:, (f % 2) * wb_stride:(f % 2 + 1) * wb_stride],
                        in_=w_dram[:, f, :, :]).then_inc(wd.h, 16)

    @ph.on("tensor")
    def _(t):
        for (sh, sv) in (wait_sems or []):
            t.wait_ge(sh.h, sv)
        for f in range(foc):
            t.wait_ge(wd.h, wd0 + 16 * (f + 1))
            for tt in range(ntt):
                g = f * ntt + tt
                if g >= 2:
                    t.wait_ge(ev.h, ev0 + g - 1)
                ps = pss[g % 2]
                mi = None
                for fi in range(fic):
                    mi = t.matmul(ps[:, :],
                                  wbuf[:, (f % 2) * wb_stride + fi * 128:
                                       (f % 2) * wb_stride + (fi + 1) * 128],
                                  rhs_slice(fi, tt),
                                  start=(fi == 0),
                                  stop=(fi == fic - 1 and n_r1 == 0))
                for i, (lr, rr) in enumerate(rank1 or []):
                    mi = t.matmul(ps[:, :], lr(f), rr(tt),
                                  start=False, stop=(i == n_r1 - 1))
                mi.then_inc(mm.h, mm_per_group)

    for f in range(foc):
        for tt in range(ntt):
            evac(ph, f, tt, pss[(f * ntt + tt) % 2], mm_vals[(f, tt)],
                 ev_vals[(f, tt)])

    if own_ph and emit_now:
        ph.emit()
    return ph



def make_residual_evac(K, scope, x_res_win, foc, ntt, tt_cols, add_fn=None,
                       gate=None):
    mm, ev = K.sem["mm"], K.sem["ev"]
    rs = [K.sem[f"rs{i}"] for i in range(4)]
    ws = [K.sem[f"ws{i}"] for i in range(4)]
    xr = ssb(K, scope, "xr", [128, 4 * tt_cols], F32)
    xw = ssb(K, scope, "xw", [128, 4 * tt_cols], F32)
    n = foc * ntt
    rs0 = [s.v for s in rs]
    ws0 = [s.v for s in ws]
    for g in range(n):
        rs[g % 4].reserve(16)
        ws[g % 4].reserve(16)
    ws_fin = [s.v for s in ws]

    def win(f, tt):
        return x_res_win[f * 128:(f + 1) * 128,
                         tt * tt_cols:(tt + 1) * tt_cols]

    state = {"preloaded": False}

    def evac(ph, f, tt, ps, mm_val, ev_val):
        g = f * ntt + tt
        if not state["preloaded"]:
            state["preloaded"] = True

            @ph.on("sync")
            def _(sy):
                if gate is not None:
                    sy.wait_ge(gate[0].h, gate[1])
                for gg in range(min(4, n)):
                    ff, tt2 = gg // ntt, gg % ntt
                    sy.dma_start(out=xr[:, (gg % 4) * tt_cols:(gg % 4 + 1) * tt_cols],
                                 in_=win(ff, tt2)).then_inc(rs[gg % 4].h, 16)

        @ph.on("vector")
        def _(v):
            s4, k = g % 4, g // 4
            v.wait_ge(rs[s4].h, rs0[s4] + 16 * (k + 1))
            if g >= 4:
                v.wait_ge(ws[s4].h, ws0[s4] + 16 * k)
            v.wait_ge(mm.h, mm_val)
            if add_fn is None:
                ins = v.tensor_tensor(out=xw[:, (g % 4) * tt_cols:(g % 4 + 1) * tt_cols], in0=ps[:, :],
                                      in1=xr[:, (g % 4) * tt_cols:(g % 4 + 1) * tt_cols], op=ALU.add)
            else:
                ins = add_fn(v, ps, xr[:, (g % 4) * tt_cols:(g % 4 + 1) * tt_cols], xw[:, (g % 4) * tt_cols:(g % 4 + 1) * tt_cols], f, tt)
            ins.then_inc(ev.h, 1)

        @ph.on("sync")
        def _(sy):
            sy.wait_ge(ev.h, ev_val)
            sy.dma_start(out=win(f, tt), in_=xw[:, (g % 4) * tt_cols:(g % 4 + 1) * tt_cols]).then_inc(ws[g % 4].h, 16)
            gg = g + 4
            if gg < n:
                ff, tt2 = gg // ntt, gg % ntt
                sy.dma_start(out=xr[:, (gg % 4) * tt_cols:(gg % 4 + 1) * tt_cols],
                             in_=win(ff, tt2)).then_inc(rs[gg % 4].h, 16)

    def finish(ph):
        @ph.on("sync")
        def _(sy):
            for s4 in range(4):
                if ws_fin[s4] > ws0[s4]:
                    sy.wait_ge(ws[s4].h, ws_fin[s4])

    return evac, finish



def sub_ffn(K, inp, x_res):
    w0 = K.input("ffn_w0", pack_w(np.asarray(inp["L0_ffn_W0"]).T))
    w1 = K.input("ffn_w1", pack_w(np.asarray(inp["L0_ffn_W1"]).T))
    b0a = K.input("ffn_b0", row(inp["L0_ffn_bias0"]))
    b1a = K.input("ffn_b1", row(inp["L0_ffn_bias1"]))
    DH = 8192
    FH = DH // 128
    ld, st, mm, ev, a2 = (K.sem[n] for n in ["ld", "st", "mm", "ev", "a2"])

    with ExitStack() as outer:
        b0r = ssb(K, outer, "b0r", [1, DH], BF16)
        b1r = ssb(K, outer, "b1r", [1, D], BF16)
        ph0 = Phase(K, "ffn_rows")
        tgt = ld.reserve(32)

        @ph0.on("gpsimd")
        def _(g):
            g.dma_start(out=b0r[0:1, :], in_=b0a[0:1, :]).then_inc(ld.h, 16)
            g.dma_start(out=b1r[0:1, :], in_=b1a[0:1, :]).then_inc(ld.h, 16)
            g.wait_ge(ld.h, tgt)
        ph0.emit()

        d_dram = K.scratch("ffn_d", [DH, T], BF16)

        with ExitStack() as s1:
            ln = emit_ln_xn(K, s1, x_res)
            stage = ssb(K, s1, "dstage", [128, 2 * T], BF16)
            ws = [K.sem["ws0"], K.sem["ws1"]]
            ws0 = [s.v for s in ws]
            for f in range(FH):
                ws[f % 2].reserve(16)
            ws_fin = [s.v for s in ws]

            def evac1(ph, f, tt, ps, mm_val, ev_val):
                @ph.on("scalar")
                def _(s):
                    s.wait_ge(mm.h, mm_val)
                    if f >= 2 and tt == 0:
                        s.wait_ge(ws[f % 2].h, ws0[f % 2] + 16 * (f // 2))
                    s.activation(out=stage[:, (f % 2) * T + tt * TT:
                                           (f % 2) * T + (tt + 1) * TT],
                                 in_=ps[:, :], func=AF.Gelu).then_inc(ev.h, 1)

                if tt == NTT - 1:
                    @ph.on("sync")
                    def _(sy):
                        sy.wait_ge(ev.h, ev_val)
                        sy.dma_start(out=d_dram[f * 128:(f + 1) * 128, :],
                                     in_=stage[:, (f % 2) * T:(f % 2 + 1) * T]
                                     ).then_inc(ws[f % 2].h, 16)

            emit_proj(K, s1, "ffn1", w_dram=w0, fic=NFT, foc=FH,
                      rhs_slice=lambda fi, tt: ln.xn[:, fi * T + tt * TT:
                                                     fi * T + (tt + 1) * TT],
                      ntt=NTT, tt_cols=TT, evac=evac1,
                      rank1=[(lambda f: b0r[0:1, f * 128:(f + 1) * 128],
                              lambda tt: K.C.ones_row[0:1, 0:TT])],
                      wait_sems=[(a2, ln.xn_done)])

        TB = 1024
        NT2 = TB // TT
        for tb in range(2):
            with ExitStack() as s2:
                darena = ssb(K, s2, "darena", [128, FH * TB], BF16)
                ph_ld = Phase(K, f"ffn2_ld_{tb}")
                ld0 = ld.v
                tgt = ld.reserve(16 * FH)

                @ph_ld.on("sync")
                def _(sy, tb=tb, ld0=ld0, tgt=tgt):
                    if tb == 0:
                        for s4 in range(2):
                            if ws_fin[s4] > ws0[s4]:
                                sy.wait_ge(ws[s4].h, ws_fin[s4])
                    for f in range(FH):
                        sy.dma_start(out=darena[:, f * TB:(f + 1) * TB],
                                     in_=d_dram[f * 128:(f + 1) * 128,
                                                tb * TB:(tb + 1) * TB]
                                     ).then_inc(ld.h, 16)
                    sy.wait_ge(ld.h, tgt)
                ph_ld.emit()

                xwin = x_res[:, tb * TB:(tb + 1) * TB]
                evac2, fin2 = make_residual_evac(K, s2, xwin, NFT, NT2, TT)
                ph2 = emit_proj(K, s2, f"ffn2_{tb}", w_dram=w1, fic=FH, foc=NFT,
                                rhs_slice=lambda fi, tt: darena[:, fi * TB + tt * TT:
                                                                fi * TB + (tt + 1) * TT],
                                ntt=NT2, tt_cols=TT, evac=evac2,
                                rank1=[(lambda f: b1r[0:1, f * 128:(f + 1) * 128],
                                        lambda tt: K.C.ones_row[0:1, 0:TT])],
                                emit_now=False)
                fin2(ph2)
                ph2.emit()



def emit_sq_norm_row(K, ph, *, n_dc, get_tile, out_row, psn4, sq2, gate=None):
    mm, a1, a2 = K.sem["mm"], K.sem["a1"], K.sem["a2"]
    mm0, a10, a20 = mm.v, a1.v, a2.v
    a1.reserve(NTT * n_dc)
    mm.reserve(NTT * n_dc)
    a2.reserve(1)

    @ph.on("scalar")
    def _(s):
        for w in (gate or []):
            s.wait_ge(w[0].h, w[1])
        for dc in range(n_dc):
            waits, tile = get_tile(dc)
            for (sh, sv) in waits:
                s.wait_ge(sh.h, sv)
            for tt in range(NTT):
                g = dc * NTT + tt
                if g >= 2:
                    s.wait_ge(mm.h, mm0 + g - 1)
                s.activation(out=sq2[:, (g % 2) * TT:(g % 2 + 1) * TT],
                             in_=tile[:, tt * TT:(tt + 1) * TT],
                             func=AF.Square).then_inc(a1.h, 1)

    @ph.on("tensor")
    def _(t):
        for dc in range(n_dc):
            for tt in range(NTT):
                g = dc * NTT + tt
                t.wait_ge(a1.h, a10 + g + 1)
                t.matmul(psn4[tt][0:1, :], K.C.ones_col[:, 0:1],
                         sq2[:, (g % 2) * TT:(g % 2 + 1) * TT],
                         start=(dc == 0), stop=(dc == n_dc - 1)).then_inc(mm.h, 1)

    @ph.on("vector")
    def _(v):
        v.wait_ge(mm.h, mm0 + n_dc * NTT)
        for tt in range(NTT):
            ins = v.tensor_copy(out_row[0:1, tt * TT:(tt + 1) * TT],
                                psn4[tt][0:1, :])
        ins.then_inc(a2.h, 1)
    return a2.v


def emit_attention(K, x_res, cfg):
    nh, hd, dk = cfg["nh"], cfg["hd"], cfg["dk"]
    tag = cfg["tag"]
    n_dc = dk // 128
    n_kv = dk // hd
    gsz = nh // n_kv
    KCP = T // 128
    dq = nh * hd
    ld, wd, st, mm, ev, a1, a2, a3, cc, xr = (
        K.sem[n] for n in ["ld", "wd", "st", "mm", "ev", "a1", "a2", "a3",
                           "cc", "xr"])

    beta_ap = K.input("beta_row", [
        row(np.full(T, 0.0 if (c % 2 == 1) else NEG, np.float32))
        for c in range(NCORES)])
    dmask_ap = K.input("c_diagmask", const_host_arrays()["c_diagmask"])

    q_dram = K.scratch(f"{tag}_q", [dq, T], BF16)
    o_dram = K.scratch(f"{tag}_o", [dq, T], BF16)
    k_own = K.scratch(f"{tag}_k_own", [dk, T], BF16)
    v_own = K.scratch(f"{tag}_v_own", [T, dk], BF16)
    ag_k = K.scratch(f"{tag}_ag_k", [2 * dk, T], BF16)
    ag_v = K.scratch(f"{tag}_ag_v", [2 * T, dk], BF16)
    K.dbg_dump(tag + "_q", q_dram, [dq, T], BF16)
    K.dbg_dump(tag + "_k", k_own, [dk, T], BF16)
    K.dbg_dump(tag + "_v", v_own, [T, dk], BF16)
    K.dbg_dump(tag + "_o", o_dram, [dq, T], BF16)
    K.dbg_dump(tag + "_agk", ag_k, [2 * dk, T], BF16)

    with ExitStack() as scope:
        ln = emit_ln_xn(K, scope, x_res, mhsa_gb=cfg.get("mhsa_gb_aps"))

        rows = {}
        for nm, width in [("bq", dq), ("bk", dk), ("bv", dk)]:
            if cfg.get(nm) is not None:
                rows[nm] = ssb(K, scope, nm, [1, width], BF16)
        if rows:
            ph0 = Phase(K, tag + "_rows")
            tgt = ld.reserve(16 * len(rows))

            @ph0.on("gpsimd")
            def _(g):
                for nm, buf in rows.items():
                    g.dma_start(out=buf[0:1, :],
                                in_=cfg[nm][0:1, :]).then_inc(ld.h, 16)
                g.wait_ge(ld.h, tgt)
            ph0.emit()

        store_fins = []

        def proj_to_dram(name, w_ap, foc, dst, bias_row, scale):
            stage = ssb(K, scope, name + "_st", [128, 2 * T], BF16)
            ws = [K.dsem(name + "_w0"), K.dsem(name + "_w1")]
            ws0 = [s.v for s in ws]
            for f in range(foc):
                ws[f % 2].reserve(16)
            store_fins.extend((s, s.v) for s in ws)

            def evac(ph, f, tt, ps, mm_val, ev_val):
                @ph.on("vector")
                def _(v):
                    v.wait_ge(mm.h, mm_val)
                    if f >= 2 and tt == 0:
                        v.wait_ge(ws[f % 2].h, ws0[f % 2] + 16 * (f // 2))
                    sl = stage[:, (f % 2) * T + tt * TT:(f % 2) * T + (tt + 1) * TT]
                    if scale is None:
                        v.tensor_copy(sl, ps[:, :]).then_inc(ev.h, 1)
                    else:
                        v.tensor_scalar_mul(sl, ps[:, :], scale).then_inc(ev.h, 1)

                if tt == NTT - 1:
                    @ph.on("sync")
                    def _(sy):
                        sy.wait_ge(ev.h, ev_val)
                        sy.dma_start(out=dst[f * 128:(f + 1) * 128, :],
                                     in_=stage[:, (f % 2) * T:(f % 2 + 1) * T]
                                     ).then_inc(ws[f % 2].h, 16)

            r1 = None
            if bias_row is not None:
                r1 = [(lambda f: bias_row[0:1, f * 128:(f + 1) * 128],
                       lambda tt: K.C.ones_row[0:1, 0:TT])]
            with ExitStack() as pscope:
                emit_proj(K, pscope, name, w_dram=w_ap, fic=NFT, foc=foc,
                          rhs_slice=lambda fi, tt: ln.xn[:, fi * T + tt * TT:
                                                         fi * T + (tt + 1) * TT],
                          ntt=NTT, tt_cols=TT, evac=evac, rank1=r1,
                          wait_sems=[(a2, ln.xn_done)])

        proj_to_dram(tag + "_qp", cfg["wq"], dq // 128, q_dram,
                     rows.get("bq"), cfg["scale"])
        proj_to_dram(tag + "_kp", cfg["wk"], n_dc, k_own, rows.get("bk"), None)

        n_dvb = dk // 512
        ph_v = Phase(K, tag + "_v")
        vw = ssb(K, scope, "vw", [128, 2 * NFT * 512], BF16)
        vstage = ssb(K, scope, "vstage", [128, 2 * 512], BF16)
        psv = [sps(K, scope, "psv", [128, 512]) for _ in range(2)]
        NTC = T // 128
        mm_per_v = NFT + (1 if "bv" in rows else 0)
        wd0, mm0 = wd.v, mm.v
        ev_v0 = ev.v
        wd.reserve(16 * n_dvb)
        mm.reserve(NTC * n_dvb * mm_per_v)
        ev.reserve(NTC * n_dvb)
        wsv = [K.dsem(tag + "_vw0"), K.dsem(tag + "_vw1")]
        wsv0 = [s.v for s in wsv]
        for g in range(NTC * n_dvb):
            wsv[g % 2].reserve(16)
        store_fins.extend((s, s.v) for s in wsv)

        @ph_v.on("gpsimd")
        def _(g):
            for dvb in range(n_dvb):
                if dvb >= 2:
                    g.wait_ge(ev.h, ev_v0 + (dvb - 1) * NTC)
                g.dma_start(
                    out=vw[:, (dvb % 2) * NFT * 512:(dvb % 2 + 1) * NFT * 512
                           ].rearrange("p (c n) -> p c n", c=NFT),
                    in_=cfg["wv_rhs"][:, :, dvb * 512:(dvb + 1) * 512]
                ).then_inc(wd.h, 16)

        @ph_v.on("tensor")
        def _(t):
            t.wait_ge(a2.h, ln.xn_done)
            for dvb in range(n_dvb):
                t.wait_ge(wd.h, wd0 + 16 * (dvb + 1))
                for tc in range(NTC):
                    g = dvb * NTC + tc
                    if g >= 2:
                        t.wait_ge(ev.h, ev_v0 + g - 1)
                    ps = psv[g % 2]
                    mi = None
                    for fi in range(NFT):
                        mi = t.matmul(ps[:, :],
                                      ln.xn[:, fi * T + tc * 128:
                                            fi * T + (tc + 1) * 128],
                                      vw[:, (dvb % 2) * NFT * 512 + fi * 512:
                                         (dvb % 2) * NFT * 512 + (fi + 1) * 512],
                                      start=(fi == 0),
                                      stop=(fi == NFT - 1 and "bv" not in rows))
                    if "bv" in rows:
                        mi = t.matmul(ps[:, :], K.C.ones_row[0:1, 0:128],
                                      rows["bv"][0:1, dvb * 512:(dvb + 1) * 512],
                                      start=False, stop=True)
                    mi.then_inc(mm.h, mm_per_v)

        @ph_v.on("scalar")
        def _(s):
            for g in range(n_dvb * NTC):
                s.wait_ge(mm.h, mm0 + (g + 1) * mm_per_v)
                if g >= 2:
                    s.wait_ge(wsv[g % 2].h, wsv0[g % 2] + 16 * (g // 2))
                s.activation(out=vstage[:, (g % 2) * 512:(g % 2 + 1) * 512],
                             in_=psv[g % 2][:, :],
                             func=AF.Copy).then_inc(ev.h, 1)

        @ph_v.on("sync")
        def _(sy):
            for dvb in range(n_dvb):
                for tc in range(NTC):
                    g = dvb * NTC + tc
                    sy.wait_ge(ev.h, ev_v0 + g + 1)
                    sy.dma_start(out=v_own[tc * 128:(tc + 1) * 128,
                                           dvb * 512:(dvb + 1) * 512],
                                 in_=vstage[:, (g % 2) * 512:(g % 2 + 1) * 512]
                                 ).then_inc(wsv[g % 2].h, 16)
        ph_v.emit()
        qkv_fins = list(store_fins)

    ph_cc = Phase(K, tag + "_cc")
    cc0 = cc.v
    cc.reserve(2)

    @ph_cc.on("gpsimd")
    def _(g):
        for sem, fin in qkv_fins:
            g.wait_ge(sem.h, fin)
        g.collective_compute("AllGather", ALU.bypass,
                             replica_groups=[[0, 1], [2, 3], [4, 5], [6, 7]],
                             ins=[k_own[:, :]], outs=[ag_k[:, :]]
                             ).then_inc(cc.h, 1)
        g.collective_compute("AllGather", ALU.bypass,
                             replica_groups=[[0, 1], [2, 3], [4, 5], [6, 7]],
                             ins=[v_own[:, :]], outs=[ag_v[:, :]]
                             ).then_inc(cc.h, 1)
        g.wait_ge(cc.h, cc0 + 2)
    ph_cc.emit()
    cc_done = cc.v

    beta_col_ap = K.input(tag + "_beta_col", [
        np.full((128, 1), (0.0 if (c % 2 == 1) else NEG) - 1.0, np.float32)
        for c in range(NCORES)])
    with ExitStack() as scope:
        ph_n = Phase(K, tag + "_norms")
        spt = 128 // hd
        qmw = ssb(K, scope, "qmw", [128, nh], F32)
        kmw = ssb(K, scope, "kmw", [128, n_kv], F32)
        qsw = ssb(K, scope, "qsw", [128, nh], F32)
        ksw = ssb(K, scope, "ksw", [128, n_kv], F32)
        kexpw = ssb(K, scope, "kexpw", [128, nh], F32)
        tmpw = ssb(K, scope, "tmpw", [128, nh], F32)
        mcol = ssb(K, scope, "mcol", [128, 2], F32)
        beta1c = ssb(K, scope, "beta1c", [128, 1], F32)
        sq2 = ssb(K, scope, "sq2", [128, 2 * T], BF16)
        ntile = ssb(K, scope, "ntile", [128, 2 * T], BF16)
        nbcols = ssb(K, scope, "nbcols", [128, 2 * nh], F32)
        nscope = ExitStack()
        psn = [sps(K, nscope, "psn", [128, TT]) for _ in range(2)]

        tiles = ([(q_dram, dc, 0) for dc in range(dq // 128)]
                 + [(k_own, dc, 1) for dc in range(n_dc)]
                 + [(ag_k, dc, 1) for dc in range(n_dc)])
        total_tiles = len(tiles)
        ld_n0 = ld.v
        ld.reserve(16)
        ns = [K.dsem(tag + "_nt0"), K.dsem(tag + "_nt1")]
        ns0 = [s.v for s in ns]
        for i in range(total_tiles):
            ns[i % 2].reserve(16)
        mm_n0, a1_n0, a2_n0 = mm.v, a1.v, a2.v
        n_groups = total_tiles * spt * NTT
        a1.reserve(total_tiles)
        a3_ones = a3.reserve(1)
        mm.reserve(n_groups)
        a2.reserve(n_groups)
        ld_beta_tgt = ld_n0 + 16

        @ph_n.on("sync")
        def _(sy):
            sy.wait_ge(cc.h, cc_done)
            for i, (d, dc, kind) in enumerate(tiles):
                if i >= 2:
                    sy.wait_ge(a1.h, a1_n0 + i - 1)
                sy.dma_start(out=ntile[:, (i % 2) * T:(i % 2 + 1) * T],
                             in_=d[dc * 128:(dc + 1) * 128, :]
                             ).then_inc(ns[i % 2].h, 16)
            sy.dma_start(out=beta1c[:, :],
                         in_=beta_col_ap[:, :]).then_inc(ld.h, 16)

        @ph_n.on("vector")
        def _(v):
            v.sem_inc(a3.h, 1)

        @ph_n.on("scalar")
        def _(s):
            for i in range(total_tiles):
                s.wait_ge(ns[i % 2].h, ns0[i % 2] + 16 * (i // 2 + 1))
                if i >= 2:
                    s.wait_ge(mm.h, mm_n0 + (i - 1) * spt * NTT)
                s.activation(out=sq2[:, (i % 2) * T:(i % 2 + 1) * T],
                             in_=ntile[:, (i % 2) * T:(i % 2 + 1) * T],
                             func=AF.Square).then_inc(a1.h, 1)

        @ph_n.on("tensor")
        def _(t):
            g = 0
            t.wait_ge(a3.h, a3_ones)
            for i in range(total_tiles):
                t.wait_ge(a1.h, a1_n0 + i + 1)
                for sp in range(spt):
                    base = sp * hd
                    for tt in range(NTT):
                        if g >= 2:
                            t.wait_ge(a2.h, a2_n0 + g - 1)
                        t.matmul(psn[g % 2][:, :],
                                 K.C.allones[base:base + hd, :],
                                 sq2[base:base + hd,
                                     (i % 2) * T + tt * TT:
                                     (i % 2) * T + (tt + 1) * TT],
                                 start=True, stop=True).then_inc(mm.h, 1)
                        g += 1

        @ph_n.on("vector")
        def _(v):
            g = 0
            seen = set()
            for i, (d, dc, kind) in enumerate(tiles):
                for sp in range(spt):
                    hidx = dc * spt + sp
                    dst = qmw if kind == 0 else kmw
                    for tt in range(NTT):
                        v.wait_ge(mm.h, mm_n0 + g + 1)
                        v.reduce_max(mcol[:, 0:1], psn[g % 2][:, :], axis=AX.X)
                        key = (kind, hidx)
                        if key not in seen:
                            seen.add(key)
                            v.tensor_copy(dst[:, hidx:hidx + 1],
                                          mcol[:, 0:1]).then_inc(a2.h, 1)
                        else:
                            v.tensor_tensor(out=dst[:, hidx:hidx + 1],
                                            in0=dst[:, hidx:hidx + 1],
                                            in1=mcol[:, 0:1],
                                            op=ALU.max).then_inc(a2.h, 1)
                        g += 1

        n_grp_eff = n_groups
        a1_r = a1.reserve(1)
        a2_r = a2.reserve(1)
        a3_r = a3.reserve(1)
        ev_r = ev.reserve(1)

        @ph_n.on("scalar")
        def _(s):
            s.wait_ge(a2.h, a2_n0 + n_grp_eff)
            s.activation(out=qsw[:, :], in_=qmw[:, :], func=AF.Sqrt)
            s.activation(out=ksw[:, :], in_=kmw[:, :],
                         func=AF.Sqrt).then_inc(a1.h, 1)

        @ph_n.on("vector")
        def _(v):
            v.wait_ge(a1.h, a1_r)
            for kv in range(n_kv):
                ins = v.tensor_scalar(out=kexpw[:, kv * gsz:(kv + 1) * gsz],
                                      in0=K.C.allones[:, 0:gsz],
                                      scalar1=ksw[:, kv:kv + 1], scalar2=None,
                                      op0=ALU.mult)
            ins.then_inc(a2.h, 1)

        @ph_n.on("gpsimd")
        def _(g):
            g.wait_ge(a2.h, a2_r)
            g.tensor_tensor(out=tmpw[:, :], in0=qsw[:, :], in1=kexpw[:, :],
                            op=ALU.mult).then_inc(a3.h, 1)

        a2_pre = a2.reserve(1)

        @ph_n.on("vector")
        def _(v):
            v.wait_ge(a3.h, a3_r)
            v.wait_ge(ld.h, ld_beta_tgt)
            v.tensor_scalar(out=nbcols[:, nh:2 * nh], in0=tmpw[:, :],
                            scalar1=-1.0, scalar2=beta1c[:, 0:1],
                            op0=ALU.mult, op1=ALU.add).then_inc(a2.h, 1)

        @ph_n.on("scalar")
        def _(s):
            s.wait_ge(a2.h, a2_pre)
            s.activation(out=nbcols[:, 0:nh], in_=tmpw[:, :], func=AF.Copy,
                         scale=-1.0, bias=-1.0).then_inc(ev.h, 1)
        ph_n.emit()
        if os.environ.get("MK_DEBUG"):
            dbg_dram = K.scratch(tag + "_dbg2", [8, 128], F32)
            K.dbg_dump(tag + "_d2", dbg_dram, [8, 128], F32)
            ph_d = Phase(K, tag + "_dbgphase")
            st_d = st.reserve(16 * 8)

            @ph_d.on("sync")
            def _(sy):
                sy.dma_start(out=dbg_dram[0:1, 0:2 * nh], in_=nbcols[0:1, :]).then_inc(st.h, 16)
                sy.dma_start(out=dbg_dram[1:2, 0:2 * nh], in_=nbcols[64:65, :]).then_inc(st.h, 16)
                sy.dma_start(out=dbg_dram[2:3, 0:nh], in_=qmw[0:1, :]).then_inc(st.h, 16)
                sy.dma_start(out=dbg_dram[3:4, 0:n_kv], in_=kmw[0:1, :]).then_inc(st.h, 16)
                sy.dma_start(out=dbg_dram[4:5, 0:nh], in_=qsw[0:1, :]).then_inc(st.h, 16)
                sy.dma_start(out=dbg_dram[5:6, 0:n_kv], in_=ksw[0:1, :]).then_inc(st.h, 16)
                sy.dma_start(out=dbg_dram[6:7, 0:nh], in_=kexpw[0:1, :]).then_inc(st.h, 16)
                sy.dma_start(out=dbg_dram[7:8, 0:nh], in_=tmpw[0:1, :]).then_inc(st.h, 16)
                sy.wait_ge(st.h, st_d)
            ph_d.emit()
        nscope.close()
        norms_done_sem = ev
        norms_done = ev_r

        ph_a = Phase(K, tag + "_core")
        dmaskb = ssb(K, scope, "dmaskb", [128, 4 * 512], BF16)
        qh = ssb(K, scope, "qh", [128, 2 * T], BF16)
        kph = ssb(K, scope, "kph", [128, 2 * T], BF16)
        koh = ssb(K, scope, "koh", [128, 2 * T], BF16)
        varena = ssb(K, scope, "varena", [128, 2 * 32 * 512], BF16)
        e_sb = ssb(K, scope, "e_sb", [128, 2 * 512], BF16)
        linv = ssb(K, scope, "linv", [1, 512], F32)
        linv_bf = ssb(K, scope, "linv_bf", [1, 512], BF16)
        invlb = ssb(K, scope, "invlb", [128, 512], BF16)
        ostage = ssb(K, scope, "ostage", [128, 2 * 512], BF16)
        ps_s = [sps(K, scope, "ps_s", [128, 512]) for _ in range(2)]
        ps_o = sps(K, scope, "ps_o", [128, 512])
        ps_l = sps(K, scope, "ps_l", [128, 512])
        ps_b2 = sps(K, scope, "ps_b2", [128, 512])

        wd_m0 = wd.v
        tgt_mask = wd.reserve(16)

        kcmax = lambda qb: KCP + 4 * (qb + 1)
        groups = [(h, qb) for h in range(nh) for qb in range(NTT)]
        NG = len(groups)

        def kv_head(h):
            return h // gsz

        def vblock(h):
            return (kv_head(h) * hd) // 512

        vb_needed = sorted(set(vblock(h) for h in range(nh)))
        vb_last_group = {vb: max(gi for gi, (h, _) in enumerate(groups)
                                 if vblock(h) == vb) for vb in vb_needed}

        mm_a0, a1_a0, a2_a0, a3_a0, ev_a0 = mm.v, a1.v, a2.v, a3.v, ev.v
        plan = []
        cm = mm_a0
        ca = a1_a0
        for gi, (h, qb) in enumerate(groups):
            nk = kcmax(qb)
            info = {"mm_s": [], "mm_lav": [], "a1_exp": []}
            for kc in range(nk):
                cm += 1
                info["mm_s"].append(cm)
                if kc >= 1:
                    cm += 1
                    info["mm_lav"].append(cm)
            cm += 1
            info["mm_lav"].append(cm)
            for kc in range(nk):
                ca += 1
                info["a1_exp"].append(ca)
            cm += 1
            info["mm_bcast"] = cm
            plan.append(info)
        mm.reserve(cm - mm_a0)
        a1.reserve(ca - a1_a0)
        a2.reserve(NG)
        a3.reserve(NG)
        ev.reserve(NG)
        osm = [K.dsem(tag + "_os0"), K.dsem(tag + "_os1")]
        osm0 = [s.v for s in osm]
        for gi in range(NG):
            osm[gi % 2].reserve(16)
        osm_fin = [s.v for s in osm]

        hs = [K.dsem(tag + "_hl0"), K.dsem(tag + "_hl1")]
        hs0 = [s.v for s in hs]
        for h in range(nh):
            hs[h % 2].reserve(48)
        wd_v0 = wd.v
        wd.reserve(16 * 32 * len(vb_needed))

        @ph_a.on("gpsimd")
        def _(g):
            g.dma_start(out=dmaskb[:, :].rearrange("p (j q) -> p j q", j=4),
                        in_=dmask_ap.rearrange("(j p) q -> p j q", p=128)
                        ).then_inc(wd.h, 16)
            for bi, vb in enumerate(vb_needed):
                if bi >= 2:
                    g.wait_ge(ev.h, ev_a0 + vb_last_group[vb_needed[bi - 2]] + 1)
                for kc in range(32):
                    src = ag_v if kc < KCP else v_own
                    r0 = (kc if kc < KCP else kc - KCP) * 128
                    g.dma_start(
                        out=varena[:, (bi % 2) * 32 * 512 + kc * 512:
                                   (bi % 2) * 32 * 512 + (kc + 1) * 512],
                        in_=src[r0:r0 + 128, vb * 512:(vb + 1) * 512]
                    ).then_inc(wd.h, 16)

        @ph_a.on("sync")
        def _(sy):
            def head_loads(h):
                kv = kv_head(h)
                sem = hs[h % 2].h
                sy.dma_start(out=qh[:, (h % 2) * T:(h % 2) * T + T][0:hd, :],
                             in_=q_dram[h * hd:(h + 1) * hd, :]).then_inc(sem, 16)
                sy.dma_start(out=koh[:, (h % 2) * T:(h % 2) * T + T][0:hd, :],
                             in_=k_own[kv * hd:(kv + 1) * hd, :]).then_inc(sem, 16)
                sy.dma_start(out=kph[:, (h % 2) * T:(h % 2) * T + T][0:hd, :],
                             in_=ag_k[kv * hd:(kv + 1) * hd, :]).then_inc(sem, 16)

            head_loads(0)
            if nh > 1:
                head_loads(1)
            for gi, (h, qb) in enumerate(groups):
                sy.wait_ge(ev.h, ev_a0 + gi + 1)
                sy.dma_start(out=o_dram[h * hd:(h + 1) * hd,
                                        qb * TT:(qb + 1) * TT],
                             in_=ostage[:, (gi % 2) * 512:(gi % 2) * 512 + 512
                                        ][0:hd, :]).then_inc(osm[gi % 2].h, 16)
                if qb == NTT - 1 and h + 2 < nh:
                    head_loads(h + 2)

        def k_slice(h, kc):
            buf = kph if kc < KCP else koh
            c = kc if kc < KCP else kc - KCP
            return buf[:, (h % 2) * T + c * 128:(h % 2) * T + (c + 1) * 128][0:hd, :]

        def q_slice(h, qb):
            return qh[:, (h % 2) * T + qb * TT:(h % 2) * T + (qb + 1) * TT][0:hd, :]

        def v_slice(h, kc):
            bi = vb_needed.index(vblock(h))
            off = (kv_head(h) * hd) % 512
            return varena[:, (bi % 2) * 32 * 512 + kc * 512 + off:
                          (bi % 2) * 32 * 512 + kc * 512 + off + hd]

        @ph_a.on("tensor")
        def _(t):
            t.wait_ge(norms_done_sem.h, norms_done)
            t.wait_ge(wd.h, tgt_mask)
            for gi, (h, qb) in enumerate(groups):
                info = plan[gi]
                nk = kcmax(qb)
                if qb == 0:
                    t.wait_ge(hs[h % 2].h, hs0[h % 2] + 48 * (h // 2 + 1))
                    bi = vb_needed.index(vblock(h))
                    t.wait_ge(wd.h, wd_v0 + 16 * 32 * (bi + 1))
                if gi >= 1:
                    t.wait_ge(ev.h, ev_a0 + gi)
                    t.wait_ge(a2.h, a2_a0 + gi)
                for kc in range(nk):
                    ps = ps_s[kc % 2]
                    if kc >= 2:
                        t.wait_ge(a1.h, info["a1_exp"][kc - 2])
                    diag_j = (kc - KCP) - 4 * qb
                    is_diag = (kc >= KCP and 0 <= diag_j < 4)
                    mi = t.matmul(ps[:, :], k_slice(h, kc), q_slice(h, qb),
                                  start=True, stop=not is_diag)
                    if is_diag:
                        mi = t.matmul(ps[:, :], K.C.identb[:, :],
                                      dmaskb[:, diag_j * 512:(diag_j + 1) * 512],
                                      start=False, stop=True)
                    mi.then_inc(mm.h, 1)
                    if kc >= 1:
                        kcl = kc - 1
                        t.wait_ge(a1.h, info["a1_exp"][kcl])
                        es = e_sb[:, (kcl % 2) * 512:(kcl % 2 + 1) * 512]
                        t.matmul(ps_l[0:1, :], K.C.ones_col[:, 0:1], es,
                                 start=(kcl == 0), stop=False)
                        t.matmul(ps_o[0:hd, :], v_slice(h, kcl), es,
                                 start=(kcl == 0), stop=False).then_inc(mm.h, 1)
                kcl = nk - 1
                t.wait_ge(a1.h, info["a1_exp"][kcl])
                es = e_sb[:, (kcl % 2) * 512:(kcl % 2 + 1) * 512]
                t.matmul(ps_l[0:1, :], K.C.ones_col[:, 0:1], es,
                         start=False, stop=True)
                t.matmul(ps_o[0:hd, :], v_slice(h, kcl), es,
                         start=False, stop=True).then_inc(mm.h, 1)
                t.wait_ge(a2.h, a2_a0 + gi + 1)
                t.matmul(ps_b2[:, :], K.C.ones_row[0:1, 0:128],
                         linv_bf[0:1, :], start=True, stop=True).then_inc(mm.h, 1)

        @ph_a.on("scalar")
        def _(s):
            for gi, (h, qb) in enumerate(groups):
                info = plan[gi]
                nk = kcmax(qb)
                for kc in range(nk):
                    s.wait_ge(mm.h, info["mm_s"][kc])
                    if os.environ.get("MK_FIXEDB"):
                        col = 1 if kc < KCP else 0
                    else:
                        col = (nh + h) if kc < KCP else h
                    s.activation(out=e_sb[:, (kc % 2) * 512:(kc % 2 + 1) * 512],
                                 in_=ps_s[kc % 2][:, :],
                                 func=AF.Exp,
                                 bias=nbcols[:, col:col + 1]).then_inc(a1.h, 1)
                s.wait_ge(mm.h, info["mm_bcast"])
                s.activation(out=invlb[:, :], in_=ps_b2[:, :],
                             func=AF.Copy).then_inc(a3.h, 1)

        @ph_a.on("vector")
        def _(v):
            for gi, (h, qb) in enumerate(groups):
                info = plan[gi]
                v.wait_ge(mm.h, info["mm_lav"][-1])
                v.reciprocal(linv[0:1, :], ps_l[0:1, :])
                v.tensor_copy(linv_bf[0:1, :], linv[0:1, :]).then_inc(a2.h, 1)
                v.wait_ge(a3.h, a3_a0 + gi + 1)
                if gi >= 2:
                    v.wait_ge(osm[gi % 2].h, osm0[gi % 2] + 16 * (gi // 2))
                v.tensor_tensor(
                    out=ostage[:, (gi % 2) * 512:(gi % 2) * 512 + 512][0:hd, :],
                    in0=ps_o[0:hd, :], in1=invlb[0:hd, :],
                    op=ALU.mult).then_inc(ev.h, 1)

        ph_a.emit()

    with ExitStack() as scope:
        oarena = ssb(K, scope, "oarena", [128, (dq // 128) * T], BF16)
        ph_ld = Phase(K, tag + "_old")
        ld0 = ld.v
        tgt = ld.reserve(16 * (dq // 128))

        @ph_ld.on("sync")
        def _(sy):
            for s4 in range(2):
                sy.wait_ge(osm[s4].h, osm_fin[s4])
            for f in range(dq // 128):
                sy.dma_start(out=oarena[:, f * T:(f + 1) * T],
                             in_=o_dram[f * 128:(f + 1) * 128, :]
                             ).then_inc(ld.h, 16)
            sy.wait_ge(ld.h, tgt)
        ph_ld.emit()

        bo_r = None
        if cfg.get("bo") is not None:
            bo_r = ssb(K, scope, "bo_r", [1, D], BF16)
            ph_b = Phase(K, tag + "_bo")
            tgt = ld.reserve(16)

            @ph_b.on("gpsimd")
            def _(g):
                g.dma_start(out=bo_r[0:1, :], in_=cfg["bo"][0:1, :]).then_inc(ld.h, 16)
                g.wait_ge(ld.h, tgt)
            ph_b.emit()

        evac_o, fin_o = make_residual_evac(K, scope, x_res, NFT, NTT, TT)
        r1 = None
        if bo_r is not None:
            r1 = [(lambda f: bo_r[0:1, f * 128:(f + 1) * 128],
                   lambda tt: K.C.ones_row[0:1, 0:TT])]
        ph_op = emit_proj(K, scope, tag + "_op", w_dram=cfg["wo"],
                          fic=dq // 128, foc=NFT,
                          rhs_slice=lambda fi, tt: oarena[:, fi * T + tt * TT:
                                                          fi * T + (tt + 1) * TT],
                          ntt=NTT, tt_cols=TT, evac=evac_o, rank1=r1,
                          emit_now=False)
        fin_o(ph_op)
        ph_op.emit()


def sub_mhsa(K, inp, x_res):
    g_ap = K.input("mhsa_lng", pcol(inp["L0_attn_b_ln_g"]))
    b_ap = K.input("mhsa_lnb", pcol(inp["L0_attn_b_ln_b"]))
    cfg = dict(
        tag="mhsa", nh=16, hd=128, dk=D, scale=1.0 / np.sqrt(128.0),
        wq=K.input("mhsa_wq", pack_w(inp["L0_attn_b_Wq"])),
        wk=K.input("mhsa_wk", pack_w(inp["L0_attn_b_Wk"])),
        wv_rhs=K.input("mhsa_wv", pack_w_rhs(inp["L0_attn_b_Wv"])),
        wo=K.input("mhsa_wo", pack_w(inp["L0_attn_b_Wo"])),
        bq=K.input("mhsa_bq", row(inp["L0_attn_b_bq"])),
        bk=K.input("mhsa_bk", row(inp["L0_attn_b_bk"])),
        bv=K.input("mhsa_bv", row(inp["L0_attn_b_bv"])),
        bo=K.input("mhsa_bo", row(inp["L0_attn_b_bo"])),
        mhsa_gb_aps=(g_ap, b_ap),
    )
    emit_attention(K, x_res, cfg)


def sub_gqa(K, inp, x_res):
    cfg = dict(
        tag="gqa", nh=32, hd=64, dk=512, scale=1.0 / np.sqrt(64.0),
        wq=K.input("gqa_wq", pack_w(np.asarray(inp["L1_attn_a_Wq"]).T)),
        wk=K.input("gqa_wk", pack_w(np.asarray(inp["L1_attn_a_Wk"]).T)),
        wv_rhs=K.input("gqa_wv", pack_w_rhs(np.asarray(inp["L1_attn_a_Wv"]).T)),
        wo=K.input("gqa_wo", pack_w(np.asarray(inp["L1_attn_a_Wo"]).T)),
    )
    emit_attention(K, x_res, cfg)



def emit_sbuf_residual_phase(K, scope, name, x_res, add_fn, foc, ntt, tt_cols,
                             pre_gate=None):
    ph = Phase(K, name)
    ev = K.sem["ev"]
    rs = [K.sem[f"rs{i}"] for i in range(4)]
    ws = [K.sem[f"ws{i}"] for i in range(4)]
    xr = ssb(K, scope, name + "_xr", [128, 4 * tt_cols], F32)
    xw = ssb(K, scope, name + "_xw", [128, 4 * tt_cols], F32)
    n = foc * ntt
    ev0 = ev.v
    ev.reserve(n)
    rs0 = [s.v for s in rs]
    ws0 = [s.v for s in ws]
    for g in range(n):
        rs[g % 4].reserve(16)
        ws[g % 4].reserve(16)
    ws_fin = [s.v for s in ws]

    def win(f, tt):
        return x_res[f * 128:(f + 1) * 128,
                     tt * tt_cols:(tt + 1) * tt_cols]

    @ph.on("sync")
    def _(sy):
        for gg in range(min(4, n)):
            sy.dma_start(out=xr[:, (gg % 4) * tt_cols:(gg % 4 + 1) * tt_cols],
                         in_=win(gg // ntt, gg % ntt)).then_inc(rs[gg % 4].h, 16)
        for g in range(n):
            f, tt = g // ntt, g % ntt
            sy.wait_ge(ev.h, ev0 + g + 1)
            sy.dma_start(out=win(f, tt),
                         in_=xw[:, (g % 4) * tt_cols:(g % 4 + 1) * tt_cols]
                         ).then_inc(ws[g % 4].h, 16)
            if g + 4 < n:
                gg = g + 4
                sy.dma_start(out=xr[:, (gg % 4) * tt_cols:(gg % 4 + 1) * tt_cols],
                             in_=win(gg // ntt, gg % ntt)).then_inc(rs[gg % 4].h, 16)
        for s4 in range(4):
            if ws_fin[s4] > ws0[s4]:
                sy.wait_ge(ws[s4].h, ws_fin[s4])

    @ph.on("vector")
    def _(v):
        if pre_gate is not None:
            v.wait_ge(pre_gate[0].h, pre_gate[1])
        for g in range(n):
            f, tt = g // ntt, g % ntt
            s4, k = g % 4, g // 4
            v.wait_ge(rs[s4].h, rs0[s4] + 16 * (k + 1))
            if g >= 4:
                v.wait_ge(ws[s4].h, ws0[s4] + 16 * k)
            ins = add_fn(v, g, f, tt,
                         xr[:, (g % 4) * tt_cols:(g % 4 + 1) * tt_cols],
                         xw[:, (g % 4) * tt_cols:(g % 4 + 1) * tt_cols])
            ins.then_inc(ev.h, 1)
    ph.emit()



def sub_moe(K, inp, x_res):
    E = 8
    rw = K.input("moe_rw", pack_w_rhs(inp["L1_ffn_router_w"]))
    rb = K.input("moe_rb", row(inp["L1_ffn_router_b"]))
    we_host = np.stack([pack_w_rhs(np.asarray(inp["L1_ffn_expert_w"])[e])
                        for e in range(E)])
    we = K.input("moe_we", we_host)
    be = K.input("moe_be", row(np.asarray(inp["L1_ffn_expert_b"]).reshape(-1)))
    pw = K.input("moe_pw", pack_w(inp["L1_ffn_proj_w"]))
    pb = K.input("moe_pb", row(inp["L1_ffn_proj_b"]))
    nw = K.input("moe_nw", pcol(inp["L1_ffn_norm_w"]))
    NTC = T // 128
    ld, wd, st, mm, ev, a1, a2, a3 = (
        K.sem[n] for n in ["ld", "wd", "st", "mm", "ev", "a1", "a2", "a3"])

    with ExitStack() as scope:
        ln = emit_ln_xn(K, scope, x_res)
        rbr = ssb(K, scope, "rbr", [1, E], BF16)
        ber = ssb(K, scope, "ber", [1, E * 512], BF16)
        pbr = ssb(K, scope, "pbr", [1, D], BF16)
        nwc = ssb(K, scope, "nwc", [128, NFT], F32)
        g_tm = ssb(K, scope, "g_tm", [128, NTC * E], F32)
        ph0 = Phase(K, "moe_rows")
        tgt = ld.reserve(16 * 3)

        @ph0.on("gpsimd")
        def _(g):
            g.dma_start(out=rbr[0:1, :], in_=rb[0:1, :]).then_inc(ld.h, 16)
            g.dma_start(out=pbr[0:1, :], in_=pb[0:1, :]).then_inc(ld.h, 16)
            g.dma_start(out=nwc[:, :], in_=nw[:, :]).then_inc(ld.h, 16)
            g.wait_ge(ld.h, tgt)
        ph0.emit()

        ph_r = Phase(K, "moe_router")
        rwt = ssb(K, scope, "rwt", [128, NFT * E], BF16)
        lgt = ssb(K, scope, "lgt", [128, 2 * E], F32)
        m1 = ssb(K, scope, "m1", [128, 2], F32)
        m2 = ssb(K, scope, "m2", [128, 2], F32)
        eq1 = ssb(K, scope, "eq1", [128, 2 * E], F32)
        eq2 = ssb(K, scope, "eq2", [128, 2 * E], F32)
        msk = ssb(K, scope, "msk", [128, 2 * E], F32)
        w1c = ssb(K, scope, "w1c", [128, 2], F32)
        w2c = ssb(K, scope, "w2c", [128, 2], F32)
        w12c = ssb(K, scope, "w12c", [128, 2], F32)
        gt1 = ssb(K, scope, "gt1", [128, 2 * E], F32)
        rscope = ExitStack()
        ps_r = [sps(K, rscope, "ps_r", [128, TT]) for _ in range(2)]
        wd_r = wd.reserve(16)
        mm_r0 = mm.v
        mm.reserve(NTC * (NFT + 1))
        a1_r0 = a1.v
        a1.reserve(NTC)
        a2_r0 = a2.v
        a2.reserve(NTC * 2)
        a3_r0 = a3.v
        a3.reserve(NTC)

        @ph_r.on("gpsimd")
        def _(g):
            g.dma_start(out=rwt[:, :],
                        in_=rw.rearrange("p c e -> p (c e)")).then_inc(wd.h, 16)

        @ph_r.on("tensor")
        def _(t):
            t.wait_ge(a2.h, ln.xn_done)
            t.wait_ge(wd.h, wd_r)
            for tc in range(NTC):
                if tc >= 2:
                    t.wait_ge(a1.h, a1_r0 + tc - 1)
                ps = ps_r[tc % 2]
                for fi in range(NFT):
                    t.matmul(ps[:, 0:E], ln.xn[:, fi * T + tc * 128:
                                               fi * T + (tc + 1) * 128],
                             rwt[:, fi * E:(fi + 1) * E],
                             start=(fi == 0), stop=False)
                t.matmul(ps[:, 0:E], K.C.ones_row[0:1, 0:128], rbr[0:1, :],
                         start=False, stop=True).then_inc(mm.h, NFT + 1)

        @ph_r.on("scalar")
        def _(s):
            for tc in range(NTC):
                s.wait_ge(mm.h, mm_r0 + (tc + 1) * (NFT + 1))
                s.activation(out=lgt[:, (tc % 2) * E:(tc % 2 + 1) * E],
                             in_=ps_r[tc % 2][:, 0:E],
                             func=AF.Copy).then_inc(a1.h, 1)
                s.wait_ge(a2.h, a2_r0 + 2 * tc + 1)
                s.activation(out=w1c[:, tc % 2:tc % 2 + 1],
                             in_=m1[:, tc % 2:tc % 2 + 1],
                             func=AF.Sigmoid).then_inc(a3.h, 1)

        @ph_r.on("vector")
        def _(v):
            for tc in range(NTC):
                b = tc % 2
                sl = slice(b * E, (b + 1) * E)
                v.wait_ge(a1.h, a1_r0 + tc + 1)
                lg = lgt[:, sl]
                K.bar("vector", v, v.reduce_max(m1[:, b:b + 1], lg, axis=AX.X))
                K.bar("vector", v, v.tensor_scalar(
                    out=eq1[:, sl], in0=lg, scalar1=m1[:, b:b + 1],
                    scalar2=None, op0=ALU.is_ge))
                K.bar("vector", v, v.scalar_tensor_tensor(
                    out=msk[:, sl], in0=eq1[:, sl], scalar=NEG, in1=lg,
                    op0=ALU.mult, op1=ALU.add))
                K.bar("vector", v, v.reduce_max(m2[:, b:b + 1], msk[:, sl],
                                                axis=AX.X))
                K.bar("vector", v, v.tensor_scalar(
                    out=eq2[:, sl], in0=lg, scalar1=m2[:, b:b + 1],
                    scalar2=None, op0=ALU.is_ge))
                v.tensor_tensor(out=m1[:, b:b + 1], in0=m1[:, b:b + 1],
                                in1=m2[:, b:b + 1],
                                op=ALU.subtract).then_inc(a2.h, 1)
                v.wait_ge(a3.h, a3_r0 + tc + 1)
                K.bar("vector", v, v.tensor_scalar(
                    out=w2c[:, b:b + 1], in0=w1c[:, b:b + 1],
                    scalar1=-1.0, scalar2=1.0, op0=ALU.mult, op1=ALU.add))
                K.bar("vector", v, v.tensor_scalar(
                    out=w12c[:, b:b + 1], in0=w1c[:, b:b + 1],
                    scalar1=2.0, scalar2=-1.0, op0=ALU.mult, op1=ALU.add))
                K.bar("vector", v, v.tensor_scalar(
                    out=gt1[:, sl], in0=eq1[:, sl],
                    scalar1=w12c[:, b:b + 1], scalar2=None, op0=ALU.mult))
                v.scalar_tensor_tensor(out=g_tm[:, tc * E:(tc + 1) * E],
                                       in0=eq2[:, sl], scalar=w2c[:, b:b + 1],
                                       in1=gt1[:, sl], op0=ALU.mult,
                                       op1=ALU.add).then_inc(a2.h, 1)
        ph_r.emit()
        rscope.close()
        gates_done = a2.v
        if os.environ.get("MK_DEBUG"):
            gdbg = K.scratch("moe_gdbg", [128, NTC * E], F32)
            K.dbg_dump("moe_g", gdbg, [128, NTC * E], F32)
            st_g = st.reserve(16)
            ph_gd = Phase(K, "moe_gdump")

            @ph_gd.on("sync")
            def _(sy):
                sy.dma_start(out=gdbg[:, :], in_=g_tm[:, :]).then_inc(st.h, 16)
                sy.wait_ge(st.h, st_g)
            ph_gd.emit()

        comb_dram = K.scratch("moe_comb", [D, T], BF16)
        escope = ExitStack()
        cstage = ssb(K, escope, "cstage", [128, 2 * 128], BF16)
        acc = ssb(K, escope, "acc", [128, NTC * 512], F32)
        wbuf = ssb(K, escope, "wbuf", [128, 2 * NFT * 512], BF16)
        h_sb = ssb(K, escope, "h_sb", [128, 2 * 512], BF16)
        ps_h = [sps(K, escope, "ps_h", [128, 512]) for _ in range(2)]
        ps_t = [sps(K, escope, "ps_t", [128, 128]) for _ in range(2)]

        ph_e = Phase(K, "moe_experts")
        wd_e0, a1_e0, ev_e0, a3_e0, a2_e0 = wd.v, a1.v, ev.v, a3.v, a2.v
        wse = [K.sem["ws0"], K.sem["ws1"]]
        wse0 = [s.v for s in wse]
        for ti in range(4 * NTC * 4):
            wse[ti % 2].reserve(16)
        wse_fin = [s.v for s in wse]
        wd.reserve(16 * 4 * E)
        a1.reserve(4 * E * NTC)
        ev.reserve(4 * E * NTC)
        a2.reserve(4)
        a3.reserve(4 * NTC * 4)

        mm_vals_h = {}
        mm_vals_t = {}
        cm = mm.v
        for fob in range(4):
            for e in range(E):
                for tc in range(NTC):
                    cm += NFT + 1
                    mm_vals_h[(fob, e, tc)] = cm
            for tc in range(NTC):
                for fs in range(4):
                    cm += 1
                    mm_vals_t[(fob, tc, fs)] = cm
        mm.reserve(cm - mm.v)

        ld_e0 = ld.v
        ld.reserve(16 * 4)

        @ph_e.on("gpsimd")
        def _(g):
            for fob in range(4):
                if fob >= 1:
                    g.wait_ge(a1.h, a1_e0 + fob * E * NTC)
                g.dma_start(out=ber[0:1, :].rearrange("o (e d) -> o e d", e=E),
                            in_=be.rearrange("o (e d) -> o e d", e=E)
                            [0:1, :, fob * 512:(fob + 1) * 512]
                            ).then_inc(ld.h, 16)
                for e in range(E):
                    i = fob * E + e
                    if i >= 2:
                        g.wait_ge(ev.h, ev_e0 + (i - 1) * NTC)
                    g.dma_start(
                        out=wbuf[:, (i % 2) * NFT * 512:(i % 2 + 1) * NFT * 512
                                 ].rearrange("p (c n) -> p c n", c=NFT),
                        in_=we[e, :, :, fob * 512:(fob + 1) * 512]
                    ).then_inc(wd.h, 16)

        @ph_e.on("vector")
        def _(v):
            v.wait_ge(a2.h, gates_done)
            for fob in range(4):
                if fob >= 1:
                    v.wait_ge(a3.h, a3_e0 + fob * NTC * 4)
                v.memset(acc[:, :], 0.0).then_inc(a2.h, 1)
                for e in range(E):
                    for tc in range(NTC):
                        gidx = (fob * E + e) * NTC + tc
                        v.wait_ge(a1.h, a1_e0 + gidx + 1)
                        v.scalar_tensor_tensor(
                            out=acc[:, tc * 512:(tc + 1) * 512],
                            in0=h_sb[:, (gidx % 2) * 512:(gidx % 2 + 1) * 512],
                            scalar=g_tm[:, tc * E + e:tc * E + e + 1],
                            in1=acc[:, tc * 512:(tc + 1) * 512],
                            op0=ALU.mult, op1=ALU.add).then_inc(ev.h, 1)

        @ph_e.on("tensor")
        def _(t):
            for fob in range(4):
                t.wait_ge(a2.h, a2_e0 + fob + 1)
                t.wait_ge(ld.h, ld_e0 + 16 * (fob + 1))
                for e in range(E):
                    i = fob * E + e
                    t.wait_ge(wd.h, wd_e0 + 16 * (i + 1))
                    for tc in range(NTC):
                        gidx = i * NTC + tc
                        if gidx >= 2:
                            t.wait_ge(a1.h, a1_e0 + gidx - 1)
                        ps = ps_h[gidx % 2]
                        for fi in range(NFT):
                            t.matmul(ps[:, :],
                                     ln.xn[:, fi * T + tc * 128:
                                           fi * T + (tc + 1) * 128],
                                     wbuf[:, (i % 2) * NFT * 512 + fi * 512:
                                          (i % 2) * NFT * 512 + (fi + 1) * 512],
                                     start=(fi == 0), stop=False)
                        t.matmul(ps[:, :], K.C.ones_row[0:1, 0:128],
                                 ber[0:1, e * 512:(e + 1) * 512],
                                 start=False, stop=True
                                 ).then_inc(mm.h, NFT + 1)
                t.wait_ge(ev.h, ev_e0 + (fob + 1) * E * NTC)
                for tc in range(NTC):
                    for fs in range(4):
                        ti = fob * 64 + tc * 4 + fs
                        if ti >= 2:
                            t.wait_ge(a3.h, a3_e0 + ti - 1)
                        t.transpose(ps_t[ti % 2][:, :],
                                    acc[:, tc * 512 + fs * 128:
                                        tc * 512 + (fs + 1) * 128],
                                    K.C.identf[:, :]).then_inc(mm.h, 1)

        @ph_e.on("scalar")
        def _(s):
            for fob in range(4):
                for e in range(E):
                    for tc in range(NTC):
                        gidx = (fob * E + e) * NTC + tc
                        s.wait_ge(mm.h, mm_vals_h[(fob, e, tc)])
                        if gidx >= 2:
                            s.wait_ge(ev.h, ev_e0 + gidx - 1)
                        s.activation(out=h_sb[:, (gidx % 2) * 512:
                                              (gidx % 2 + 1) * 512],
                                     in_=ps_h[gidx % 2][:, :],
                                     func=AF.Silu).then_inc(a1.h, 1)
                for tc in range(NTC):
                    for fs in range(4):
                        s.wait_ge(mm.h, mm_vals_t[(fob, tc, fs)])
                        ti = fob * 64 + tc * 4 + fs
                        if ti >= 2:
                            s.wait_ge(wse[ti % 2].h,
                                      wse0[ti % 2] + 16 * (ti // 2))
                        s.activation(out=cstage[:, (ti % 2) * 128:
                                                (ti % 2 + 1) * 128],
                                     in_=ps_t[ti % 2][:, :],
                                     func=AF.Copy).then_inc(a3.h, 1)

        @ph_e.on("sync")
        def _(sy):
            for fob in range(4):
                for tc in range(NTC):
                    for fs in range(4):
                        ti = fob * 64 + tc * 4 + fs
                        fc = fob * 4 + fs
                        sy.wait_ge(a3.h, a3_e0 + ti + 1)
                        sy.dma_start(out=comb_dram[fc * 128:(fc + 1) * 128,
                                                   tc * 128:(tc + 1) * 128],
                                     in_=cstage[:, (ti % 2) * 128:
                                                (ti % 2 + 1) * 128]
                                     ).then_inc(wse[ti % 2].h, 16)
            for s4 in range(2):
                sy.wait_ge(wse[s4].h, wse_fin[s4])
        ph_e.emit()
        escope.close()
        comb_done = a3.v
        if os.environ.get("MK_DEBUG"):
            K.dbg_dump("moe_comb", comb_dram, [D, T], BF16)

        sqy = ssb(K, scope, "sqy", [128, 2 * TT], BF16)
        rstd_b = ssb(K, scope, "rstd_b", [128, T], F32)
        pscope = ExitStack()
        combined = ssb(K, pscope, "combarena", [128, NFT * T], BF16)
        ph_cl = Phase(K, "moe_combload")
        ld_c0 = ld.v
        tgt_c = ld.reserve(16 * NFT)

        @ph_cl.on("sync")
        def _(sy):
            for f in range(NFT):
                sy.dma_start(out=combined[:, f * T:(f + 1) * T],
                             in_=comb_dram[f * 128:(f + 1) * 128, :]
                             ).then_inc(ld.h, 16)
            sy.wait_ge(ld.h, tgt_c)
        ph_cl.emit()

        def evac_p(ph, f, tt, ps, mm_val, ev_val):
            @ph.on("vector")
            def _(v):
                v.wait_ge(mm.h, mm_val)
                v.tensor_tensor(out=ln.xn[:, f * T + tt * TT:
                                          f * T + (tt + 1) * TT],
                                in0=ps[:, :],
                                in1=ln.xn[:, f * T + tt * TT:
                                          f * T + (tt + 1) * TT],
                                op=ALU.add).then_inc(ev.h, 1)

        emit_proj(K, pscope, "moe_proj", w_dram=pw, fic=NFT, foc=NFT,
                  rhs_slice=lambda fi, tt: combined[:, fi * T + tt * TT:
                                                    fi * T + (tt + 1) * TT],
                  ntt=NTT, tt_cols=TT, evac=evac_p,
                  rank1=[(lambda f: pbr[0:1, f * 128:(f + 1) * 128],
                          lambda tt: K.C.ones_row[0:1, 0:TT])],
                  wait_sems=[(a3, comb_done)])

        ph_s = Phase(K, "moe_rms")
        ps_rms = [sps(K, pscope, "ps_rms", [128, TT]) for _ in range(4)]
        a1_s0 = a1.v
        a1.reserve(NFT * NTT)
        mm_s0 = mm.v
        mm.reserve(NFT * NTT)
        a2_s0 = a2.v
        a2.reserve(NTT)

        @ph_s.on("scalar")
        def _(s):
            g = 0
            for f in range(NFT):
                for tt in range(NTT):
                    if g >= 2:
                        s.wait_ge(mm.h, mm_s0 + g - 1)
                    s.activation(out=sqy[:, (g % 2) * TT:(g % 2 + 1) * TT],
                                 in_=ln.xn[:, f * T + tt * TT:
                                           f * T + (tt + 1) * TT],
                                 func=AF.Square).then_inc(a1.h, 1)
                    g += 1

        @ph_s.on("tensor")
        def _(t):
            g = 0
            for f in range(NFT):
                for tt in range(NTT):
                    t.wait_ge(a1.h, a1_s0 + g + 1)
                    t.matmul(ps_rms[tt][:, :], K.C.allones[:, :],
                             sqy[:, (g % 2) * TT:(g % 2 + 1) * TT],
                             start=(f == 0),
                             stop=(f == NFT - 1)).then_inc(mm.h, 1)
                    g += 1

        @ph_s.on("vector")
        def _(v):
            for tt in range(NTT):
                v.wait_ge(mm.h, mm_s0 + (NFT - 1) * NTT + tt + 1)
                v.tensor_scalar(out=rstd_b[:, tt * TT:(tt + 1) * TT],
                                in0=ps_rms[tt][:, :], scalar1=1.0 / D,
                                scalar2=1e-6, op0=ALU.mult, op1=ALU.add)
                v.reciprocal(rstd_b[:, tt * TT:(tt + 1) * TT],
                             rstd_b[:, tt * TT:(tt + 1) * TT]).then_inc(a2.h, 1)

        rstd_done = a1.reserve(1)

        @ph_s.on("scalar")
        def _(s):
            s.wait_ge(a2.h, a2_s0 + NTT)
            s.activation(out=rstd_b[:, :], in_=rstd_b[:, :],
                         func=AF.Sqrt).then_inc(a1.h, 1)
        ph_s.emit()
        pscope.close()

        if os.environ.get("MK_DEBUG"):
            ydbg = K.scratch(K.uname("moe_ydbg"), [D, T], BF16)
            rdbg = K.scratch(K.uname("moe_rdbg"), [128, T], F32)
            K.dbg_dump("moe_y", ydbg, [D, T], BF16)
            K.dbg_dump("moe_rstd", rdbg, [128, T], F32)
            st_y = st.reserve(16 * (NFT + 1))
            ph_yd = Phase(K, "moe_ydump")

            @ph_yd.on("sync")
            def _(sy):
                for f in range(NFT):
                    sy.dma_start(out=ydbg[f * 128:(f + 1) * 128, :],
                                 in_=ln.xn[:, f * T:(f + 1) * T]
                                 ).then_inc(st.h, 16)
                sy.dma_start(out=rdbg[:, :], in_=rstd_b[:, :]
                             ).then_inc(st.h, 16)
                sy.wait_ge(st.h, st_y)
            ph_yd.emit()

        scr2 = ssb(K, scope, "scr2", [128, 4 * TT], F32)

        def add_fn(v, g, f, tt, xr_ap, xw_ap):
            ysl = ln.xn[:, f * T + tt * TT: f * T + (tt + 1) * TT]
            K.bar("vector", v, v.scalar_tensor_tensor(
                out=scr2[:, (g % 4) * TT:(g % 4 + 1) * TT],
                in0=ysl, scalar=nwc[:, f:f + 1],
                in1=rstd_b[:, tt * TT:(tt + 1) * TT],
                op0=ALU.mult, op1=ALU.mult))
            return v.tensor_tensor(out=xw_ap,
                                   in0=scr2[:, (g % 4) * TT:(g % 4 + 1) * TT],
                                   in1=xr_ap, op=ALU.add)

        emit_sbuf_residual_phase(K, scope, "moe_fin", x_res, add_fn,
                                 NFT, NTT, TT,
                                 pre_gate=(a1, rstd_done))



MB_H, MB_P, MB_N, MB_CH = 64, 64, 128, 256
MB_IN = 4096
MB_CONV = 4352
MB_DIP = 8512
MB_NC = T // MB_CH


def emit_mamba_pass1(K, scope, tg, E):
    ld, st, mm, ev, a1, a2, a3 = (
        K.sem[n] for n in ["ld", "st", "mm", "ev", "a1", "a2", "a3"])
    NC, NX, NFC = MB_NC, 32, 34

    B = {}
    bscope = ExitStack()
    scope = bscope
    B["raw"] = ssb(K, scope, "raw", [128, NFC * 260], BF16)
    B["tailc"] = ssb(K, scope, "tailc", [128, NFC * 3], BF16)
    B["accv"] = ssb(K, scope, "accv", [128, NFC * MB_CH], F32)
    B["xconv"] = ssb(K, scope, "xconv", [128, NFC * MB_CH], BF16)
    B["dtraw"] = ssb(K, scope, "dtraw", [64, MB_CH], BF16)
    B["dt0"] = ssb(K, scope, "dt0", [64, MB_CH], F32)
    B["dta"] = ssb(K, scope, "dta", [64, MB_CH], F32)
    B["dtv"] = ssb(K, scope, "dtv", [64, MB_CH], F32)
    B["dAc"] = ssb(K, scope, "dAc", [64, MB_CH], F32)
    B["acs"] = ssb(K, scope, "acs", [64, MB_CH], F32)
    B["acst"] = ssb(K, scope, "acst", [128, 128], F32)
    B["dt_tm"] = ssb(K, scope, "dt_tm", [128, 128], F32)
    B["b_tm"] = ssb(K, scope, "b_tm", [128, 2 * 128], BF16)
    B["x_tm"] = ssb(K, scope, "x_tm", [128, 2 * NX * 128], BF16)
    B["gm_sb"] = ssb(K, scope, "gm_sb", [128, 2 * MB_CH], BF16)
    B["rel_sb"] = ssb(K, scope, "rel_sb", [128, 2 * MB_CH], F32)
    B["lm_sb"] = ssb(K, scope, "lm_sb", [128, 4 * MB_CH], BF16)
    B["m2_sb"] = ssb(K, scope, "m2_sb", [128, 4 * MB_CH], BF16)
    B["bds_sb"] = ssb(K, scope, "bds_sb", [128, 4 * 128], BF16)
    B["bsc"] = ssb(K, scope, "bsc", [128, 8], F32)
    B["PS"] = ssb(K, scope, "PS", [128, MB_H * MB_P], F32)
    B["ypa"] = ssb(K, scope, "ypa", [128, NX * MB_CH], BF16)
    B["eacol"] = ssb(K, scope, "eacol", [64, 1], F32)
    B["ea_row"] = ssb(K, scope, "ea_row", [1, 64], BF16)
    B["cdb_sb"] = ssb(K, scope, "cdb_sb", [128, 64], F32)
    B["dcum_sb"] = E["dcum_sb"]
    B["stbf"] = ssb(K, scope, "stbf", [128, MB_H * MB_P], BF16)
    E["B"] = B
    E["yps"] = K.dsem(tg + "_yps")
    E["yp_st_prev"] = E["yps"].v

    cs_sem, cs_val = E["consts_done"]
    ph_i = Phase(K, tg + "_p1init")
    a2_init = a2.reserve(1)

    @ph_i.on("vector")
    def _(v):
        v.wait_ge(cs_sem.h, cs_val)
        v.memset(B["PS"][:, :], 0.0)
        v.memset(B["dcum_sb"][:, :], 1.0).then_inc(a2.h, 1)

    @ph_i.on("sync")
    def _(sy):
        for sem, fin in E["st_zx_done"]:
            sy.wait_ge(sem.h, fin)
    ph_i.emit()

    for c in range(NC):
        _mb_p1_pre(K, E, c)
        _mb_p1_heads(K, E, c)

    ph_f = Phase(K, tg + "_p1fin")
    a2_f = a2.reserve(1)
    st_f = st.reserve(16)

    @ph_f.on("vector")
    def _(v):
        v.tensor_copy(B["stbf"][:, :], B["PS"][:, :]).then_inc(a2.h, 1)

    @ph_f.on("sync")
    def _(sy):
        sy.wait_ge(a2.h, a2_f)
        sy.dma_start(out=E["stf"][:, :], in_=B["stbf"][:, :]).then_inc(st.h, 16)
        sy.wait_ge(st.h, st_f)
        sy.wait_ge(E["yps"].h, E["yps"].v)
    ph_f.emit()
    E["p1_st_done"] = st.v
    bscope.close()


def _mb_p1_pre(K, E, c):
    ld, st, mm, ev, a1, a2, a3 = (
        K.sem[n] for n in ["ld", "st", "mm", "ev", "a1", "a2", "a3"])
    NC, RAWW, NX, NFC, CH = MB_NC, 260, 32, 34, MB_CH
    B = E["B"]
    zx = E["zx"]
    t0 = c * CH
    ph = Phase(K, E["tg"] + f"_p1a{c}")
    pscope = ExitStack()
    ps_tf = [sps(K, pscope, "ps_tf", [128, 128]) for _ in range(2)]
    ps_tb = [sps(K, pscope, "ps_tb", [128, 128], BF16) for _ in range(2)]
    ps_g = sps(K, pscope, "ps_g", [128, CH])

    n_ld = 5 + (1 if c == 0 else 0)
    ld0 = ld.v
    ld_all = ld.reserve(16 * n_ld)
    ld_raw = [ld_all] * 4
    ld_dt = ld_all
    ld_tail = ld_all
    st0 = st.v
    st.reserve(16 * 3)
    wd = K.sem["wd"]
    E["st_prev"] = wd.reserve(16)

    @ph.on("gpsimd")
    def _(g):
        g.dma_start(out=E["prev"][:, c * MB_H * MB_P:(c + 1) * MB_H * MB_P],
                    in_=B["PS"][:, :]).then_inc(wd.h, 16)

    @ph.on("sync")
    def _(sy):
        for grp in range(4):
            f0, f1 = grp * 9, min(NFC, grp * 9 + 9)
            if c == 0:
                sy.dma_start(
                    out=B["raw"][:, :].rearrange("p (f j) -> p f j", f=NFC)
                    [:, f0:f1, 4:RAWW],
                    in_=zx.rearrange("(f p) t -> p f t", p=128)
                    [:, 32 + f0:32 + f1, t0:t0 + CH]).then_inc(ld.h, 16)
            else:
                sy.dma_start(
                    out=B["raw"][:, :].rearrange("p (f j) -> p f j", f=NFC)
                    [:, f0:f1, 0:RAWW],
                    in_=zx.rearrange("(f p) t -> p f t", p=128)
                    [:, 32 + f0:32 + f1, t0 - 4:t0 + CH]).then_inc(ld.h, 16)
        sy.dma_start(out=B["dtraw"][:, :], in_=zx[8448:8512, t0:t0 + CH]
                     ).then_inc(ld.h, 16)
        if c == 0:
            sy.dma_start(out=B["tailc"][:, :], in_=E["ag_tail"][0:128, :]
                         ).then_inc(ld.h, 16)
        sy.wait_ge(a2.h, E["_a2_dtv"])
        sy.dma_start(out=E["acs"][:, t0:t0 + CH], in_=B["acs"][:, :]
                     ).then_inc(st.h, 16)
        sy.dma_start(out=E["dtv"][:, t0:t0 + CH], in_=B["dtv"][:, :]
                     ).then_inc(st.h, 16)
        sy.wait_ge(a1.h, E["_a1_silu"])
        sy.dma_start(out=E["xc"][:, t0:t0 + CH],
                     in_=B["xconv"][:, 33 * CH:34 * CH]).then_inc(st.h, 16)

    a2_conv = a2.reserve(1)

    def conv_stream(eng_name, sem):
        def emit(e):
            my = list(range(NFC))
            for gv in ld_raw:
                e.wait_ge(ld.h, gv)
            if c == 0:
                e.wait_ge(ld.h, ld_tail)
            if c == 0 and eng_name == "vector":
                K.bar("vector", e, e.tensor_scalar(
                    out=B["tailc"][:, :], in0=B["tailc"][:, :],
                    scalar1=E["kapc"][:, 0:1], scalar2=None,
                    op0=ALU.mult))
            if c == 0:
                for ft in my:
                    e.tensor_scalar(
                        out=B["accv"][:, ft * CH:ft * CH + 3],
                        in0=B["tailc"][:, ft * 3:ft * 3 + 3],
                        scalar1=0.0, scalar2=E["cbc"][:, ft:ft + 1],
                        op0=ALU.mult, op1=ALU.add)
            last = None
            for k in range(4):
                for ft in my:
                    lo = (3 - k) if c == 0 else 0
                    o = B["accv"][:, ft * CH + lo:(ft + 1) * CH]
                    rsl = B["raw"][:, ft * RAWW + lo + k + 1:
                                   ft * RAWW + CH + k + 1]
                    wcol = E["cwc"][:, ft * 4 + k:ft * 4 + k + 1]
                    if k == 0:
                        last = e.tensor_scalar(out=o, in0=rsl, scalar1=wcol,
                                               scalar2=E["cbc"][:, ft:ft + 1],
                                               op0=ALU.mult, op1=ALU.add)
                    else:
                        last = e.scalar_tensor_tensor(out=o, in0=rsl,
                                                      scalar=wcol, in1=o,
                                                      op0=ALU.mult, op1=ALU.add)
            if c == 0:
                for k in range(3):
                    for ft in my:
                        w = 3 - k
                        o = B["accv"][:, ft * CH:ft * CH + w]
                        tsl = B["tailc"][:, ft * 3 + k:ft * 3 + 3]
                        last = e.scalar_tensor_tensor(
                            out=o, in0=tsl,
                            scalar=E["cwc"][:, ft * 4 + k:ft * 4 + k + 1],
                            in1=o, op0=ALU.mult, op1=ALU.add)
            last.then_inc(sem.h, 1)
        return emit

    ph.on("vector")(conv_stream("vector", a2))

    a2_dta = a2.reserve(1)
    a2_dtv = a2.reserve(1)
    a2_dcum = a2.reserve(1)
    a2_gm = a2.reserve(1)
    a2_gm2 = a2.reserve(1)
    E["_a2_dtv"] = a2_dtv

    @ph.on("vector")
    def _(v):
        v.wait_ge(ld.h, ld_dt)
        K.bar("vector", v, v.tensor_scalar(out=B["dt0"][:, :],
                                           in0=B["dtraw"][:, :],
                                           scalar1=E["dtbc"][:, 0:1],
                                           scalar2=None, op0=ALU.add))
        K.bar("vector", v, v.tensor_scalar(out=B["dta"][:, :],
                                           in0=B["dt0"][:, :], scalar1=-1.0,
                                           scalar2=None, op0=ALU.mult))
        v.tensor_tensor(out=B["dta"][:, :], in0=B["dta"][:, :],
                        in1=B["dt0"][:, :], op=ALU.max).then_inc(a2.h, 1)
        K.bar("vector", v, v.tensor_scalar(out=B["dtv"][:, :],
                                           in0=B["dt0"][:, :], scalar1=0.0,
                                           scalar2=None, op0=ALU.max))
        v.wait_ge(a1.h, E["_a1_dt"])
        K.bar("vector", v, v.tensor_tensor(out=B["dtv"][:, :],
                                           in0=B["dtv"][:, :],
                                           in1=B["dta"][:, :], op=ALU.add))
        if E["clip100"]:
            K.bar("vector", v, v.tensor_scalar(out=B["dtv"][:, :],
                                               in0=B["dtv"][:, :],
                                               scalar1=100.0, scalar2=None,
                                               op0=ALU.min))
        K.bar("vector", v, v.tensor_scalar(out=B["dAc"][:, :],
                                           in0=B["dtv"][:, :],
                                           scalar1=E["acol"][:, 0:1],
                                           scalar2=None, op0=ALU.mult))
        v.tensor_tensor_scan(
            out=B["acs"][:, :], data0=E["onesf"][0:64, :],
            data1=B["dAc"][:, :], initial=0.0, op0=ALU.mult,
            op1=ALU.add).then_inc(a2.h, 1)
        v.wait_ge(a1.h, E["_a1_ea"])
        v.tensor_tensor(
            out=B["dcum_sb"][:, c + 1:c + 2], in0=B["dcum_sb"][:, c:c + 1],
            in1=B["eacol"][:, :], op=ALU.mult).then_inc(a2.h, 1)
        for sc in range(2):
            v.wait_ge(mm.h, E["_mm_g"][sc])
            v.tensor_tensor(out=B["gm_sb"][:, sc * CH:(sc + 1) * CH],
                            in0=ps_g[:, :],
                            in1=E["gmk"][:, sc * CH:(sc + 1) * CH],
                            op=ALU.mult).then_inc(a2.h, 1)

    a1_silu = a1.reserve(1)
    a1_dt = a1.reserve(1)
    a1_ea = a1.reserve(1)
    E["_a1_silu"], E["_a1_dt"], E["_a1_ea"] = a1_silu, a1_dt, a1_ea
    trs = _mb_tr_plan(B)
    NTR = len(trs)
    a1_tr0 = a1.v
    a1.reserve(NTR)
    a1_cdb = a1.reserve(1)
    E["_a1_cdb"] = a1_cdb

    mm0 = mm.v
    mm.reserve(NTR)
    mm_g = [mm.reserve(1), mm.reserve(1)]
    E["_mm_g"] = mm_g
    mm_cdb = mm.reserve(1)

    @ph.on("scalar")
    def _(s):
        s.wait_ge(a2.h, a2_conv)
        lastop = None
        for ft in range(NFC):
            lastop = s.activation(out=B["xconv"][:, ft * CH:(ft + 1) * CH],
                                  in_=B["accv"][:, ft * CH:(ft + 1) * CH],
                                  func=AF.Silu)
        lastop.then_inc(a1.h, 1)
        s.wait_ge(a2.h, a2_dta)
        K.bar("scalar", s, s.activation(out=B["dta"][:, :], in_=B["dta"][:, :],
                                        func=AF.Exp, scale=-1.0))
        s.activation(out=B["dta"][:, :], in_=B["dta"][:, :],
                     func=AF.Ln, bias=1.0).then_inc(a1.h, 1)
        s.wait_ge(a2.h, a2_dtv)
        s.activation(out=B["eacol"][:, :], in_=B["acs"][:, 255:256],
                     func=AF.Exp).then_inc(a1.h, 1)
        for i, tr in enumerate(trs):
            s.wait_ge(mm.h, mm0 + i + 1)
            ps = (ps_tf if tr["fp32"] else ps_tb)[tr["pi"]]
            s.activation(out=tr["dst"], in_=ps[0:tr["rows"], 0:tr["cols"]],
                         func=AF.Copy).then_inc(a1.h, 1)
        s.wait_ge(mm.h, mm_cdb)
        s.activation(out=B["cdb_sb"][:, :], in_=ps_g[:, 0:64],
                     func=AF.Copy).then_inc(a1.h, 1)

    @ph.on("tensor")
    def _(t):
        fi = [0, 0]
        for i, tr in enumerate(trs):
            for (sh, sv) in tr["waits"]:
                t.wait_ge(sh.h, sv)
            pool = ps_tf if tr["fp32"] else ps_tb
            k = 0 if tr["fp32"] else 1
            if fi[k] >= 2:
                t.wait_ge(a1.h, a1_tr0 + tr["prev2"] + 1)
            npart = tr["src"].shape[0]
            ident = (K.C.identf if tr["fp32"] else K.C.identb)[0:npart, 0:npart]
            t.transpose(pool[tr["pi"]][0:tr["rows"], 0:tr["cols"]],
                        tr["src"], ident).then_inc(mm.h, 1)
            fi[k] += 1
        t.wait_ge(a1.h, a1_silu)
        for sc in range(2):
            if sc == 1:
                t.wait_ge(a2.h, a2_gm)
            t.matmul(ps_g[:, :],
                     B["xconv"][:, 32 * CH + sc * 128:32 * CH + (sc + 1) * 128],
                     B["xconv"][:, 33 * CH:34 * CH],
                     start=True, stop=True).then_inc(mm.h, 1)
        t.wait_ge(a2.h, a2_gm2)
        t.wait_ge(a1.h, a1_tr0 + NTR)
        t.matmul(ps_g[:, 0:64], K.C.ones_row[0:1, 0:128],
                 B["ea_row"][0:1, :], start=True, stop=True).then_inc(mm.h, 1)

    for i, tr in enumerate(trs):
        w = []
        if tr["kind"] in ("acst", "dt"):
            w.append((a2, a2_dtv))
        elif tr["kind"] in ("b", "x"):
            w.append((a1, a1_silu))
        elif tr["kind"] == "ea":
            w.append((a1, a1_ea))
        tr["waits"] = w
    ph.emit()
    pscope.close()


def _mb_tr_plan(B):
    trs = []
    for sc in range(2):
        trs.append(dict(kind="acst", fp32=True, rows=128, cols=64,
                        src=B["acs"][0:64, sc * 128:(sc + 1) * 128],
                        dst=B["acst"][0:128, sc * 64:(sc + 1) * 64]))
    for sc in range(2):
        trs.append(dict(kind="dt", fp32=True, rows=128, cols=64,
                        src=B["dtv"][0:64, sc * 128:(sc + 1) * 128],
                        dst=B["dt_tm"][0:128, sc * 64:(sc + 1) * 64]))
    trs.append(dict(kind="ea", fp32=True, rows=1, cols=64,
                    src=B["eacol"][0:64, 0:1],
                    dst=B["ea_row"][0:1, 0:64]))
    for sc in range(2):
        trs.append(dict(kind="b", fp32=False, rows=128, cols=128,
                        src=B["xconv"][:, 32 * MB_CH + sc * 128:
                                       32 * MB_CH + (sc + 1) * 128],
                        dst=B["b_tm"][0:128, sc * 128:(sc + 1) * 128]))
    for sc in range(2):
        for ft in range(32):
            trs.append(dict(
                kind="x", fp32=False, rows=128, cols=128,
                src=B["xconv"][:, ft * MB_CH + sc * 128:
                               ft * MB_CH + (sc + 1) * 128],
                dst=B["x_tm"][0:128, sc * 32 * 128 + ft * 128:
                              sc * 32 * 128 + (ft + 1) * 128]))
    cnt = {True: 0, False: 0}
    hist = {True: [], False: []}
    for i, tr in enumerate(trs):
        k = tr["fp32"]
        tr["pi"] = cnt[k] % 2
        tr["prev2"] = hist[k][-2] if len(hist[k]) >= 2 else 0
        hist[k].append(i)
        cnt[k] += 1
    return trs


def _mb_p1_heads(K, E, c):
    ld, st, mm, ev, a1, a2, a3 = (
        K.sem[n] for n in ["ld", "st", "mm", "ev", "a1", "a2", "a3"])
    NC, NX, CH = MB_NC, 32, MB_CH
    B = E["B"]
    t0 = c * CH
    ph = Phase(K, E["tg"] + f"_p1b{c}")
    pscope = ExitStack()
    ps_r = [sps(K, pscope, "ps_r", [128, CH]) for _ in range(2)]
    ps_yd = [sps(K, pscope, "ps_yd", [128, CH]) for _ in range(2)]
    ps_s = [sps(K, pscope, "ps_s", [128, 128]) for _ in range(2)]

    mm_h0 = mm.v
    plan_R, plan_YS = [], []
    cm = mm_h0
    for h in range(MB_H):
        cm += 1
        plan_R.append(cm)
        if h >= 1:
            cm += 1
            plan_YS.append(cm)
    cm += 1
    plan_YS.append(cm)
    mm.reserve(cm - mm_h0)
    a1_h0 = a1.v
    a1.reserve(MB_H)
    a2_h0 = a2.v
    a2.reserve(MB_H)
    ev_0 = ev.v
    ev.reserve(MB_H)
    yps = E["yps"]
    yp_prev = E["yp_st_prev"]
    E["yp_st_prev"] = yps.reserve(16 * 4)
    st_prev = E["st_prev"]

    @ph.on("tensor")
    def _(t):
        for h in range(MB_H):
            if h >= 2:
                t.wait_ge(a1.h, a1_h0 + h - 1)
            t.matmul(ps_r[h % 2][:, :],
                     E["selb"][:, h * 128:(h + 1) * 128],
                     B["acs"][:, :], start=True, stop=True).then_inc(mm.h, 1)
            if h >= 1:
                _mb_yds_mms(K, t, E, h - 1, ps_yd, ps_s, a2, a2_h0, ev, ev_0,
                            mm)
        _mb_yds_mms(K, t, E, MB_H - 1, ps_yd, ps_s, a2, a2_h0, ev, ev_0, mm)

    @ph.on("scalar")
    def _(s):
        for h in range(MB_H):
            s.wait_ge(mm.h, plan_R[h])
            lastop = None
            for sc in range(2):
                col = B["acst"][:, sc * 64 + h:sc * 64 + h + 1]
                K.bar("scalar", s, s.activation(
                    out=B["rel_sb"][:, sc * CH:(sc + 1) * CH],
                    in_=ps_r[h % 2][:, :], func=AF.Relu,
                    scale=-1.0, bias=col))
                lastop = s.activation(
                    out=B["lm_sb"][:, (h % 2) * 2 * CH + sc * CH:
                                   (h % 2) * 2 * CH + (sc + 1) * CH],
                    in_=B["rel_sb"][:, sc * CH:(sc + 1) * CH],
                    func=AF.Exp, scale=-1.0)
            lastop.then_inc(a1.h, 1)

    @ph.on("vector")
    def _(v):
        for h in range(MB_H):
            v.wait_ge(a1.h, a1_h0 + h + 1)
            if h == 0:
                v.wait_ge(yps.h, yp_prev)
                v.wait_ge(K.sem["wd"].h, st_prev)
            ins = None
            for sc in range(2):
                lm = B["lm_sb"][:, (h % 2) * 2 * CH + sc * CH:
                                (h % 2) * 2 * CH + (sc + 1) * CH]
                ins = v.tensor_scalar(
                    out=B["bsc"][:, (h % 2) * 2 + sc:(h % 2) * 2 + sc + 1],
                    in0=lm[:, 255:256],
                    scalar1=B["dt_tm"][:, sc * 64 + h:sc * 64 + h + 1],
                    scalar2=None, op0=ALU.mult)
                v.scalar_tensor_tensor(
                    out=B["m2_sb"][:, (h % 2) * 2 * CH + sc * CH:
                                   (h % 2) * 2 * CH + (sc + 1) * CH],
                    in0=B["gm_sb"][:, sc * CH:(sc + 1) * CH],
                    scalar=B["dt_tm"][:, sc * 64 + h:sc * 64 + h + 1],
                    in1=lm, op0=ALU.mult, op1=ALU.mult)
            K.bar("vector", v, ins)
            ins2 = None
            for sc in range(2):
                ins2 = v.tensor_scalar(
                    out=B["bds_sb"][:, (h % 2) * 256 + sc * 128:
                                    (h % 2) * 256 + (sc + 1) * 128],
                    in0=B["b_tm"][:, sc * 128:(sc + 1) * 128],
                    scalar1=B["bsc"][:, (h % 2) * 2 + sc:
                                     (h % 2) * 2 + sc + 1],
                    scalar2=None, op0=ALU.mult)
            ins2.then_inc(a2.h, 1)
            v.wait_ge(mm.h, plan_YS[h])
            base = (h % 2) * 64
            ft = h // 2
            v.scalar_tensor_tensor(
                out=B["ypa"][base:base + 64, ft * CH:(ft + 1) * CH],
                in0=B["xconv"][base:base + 64, ft * CH:(ft + 1) * CH],
                scalar=E["dxc"][base:base + 64, ft:ft + 1],
                in1=ps_yd[h % 2][0:64, :],
                op0=ALU.mult, op1=ALU.add)
            v.scalar_tensor_tensor(
                out=B["PS"][:, h * 64:(h + 1) * 64],
                in0=B["PS"][:, h * 64:(h + 1) * 64],
                scalar=B["cdb_sb"][:, h:h + 1],
                in1=ps_s[h % 2][:, 0:64],
                op0=ALU.mult, op1=ALU.add).then_inc(ev.h, 1)

    @ph.on("sync")
    def _(sy):
        sy.wait_ge(ev.h, ev_0 + MB_H)
        for g in range(4):
            sy.dma_start(
                out=E["yp"].rearrange("(f p) t -> p f t", p=128)
                [:, g * 8:(g + 1) * 8, t0:t0 + CH],
                in_=B["ypa"][:, :].rearrange("p (f l) -> p f l", f=NX)
                [:, g * 8:(g + 1) * 8, :]).then_inc(yps.h, 16)
    ph.emit()
    pscope.close()


def _mb_yds_mms(K, t, E, hh, ps_yd, ps_s, a2, a2_h0, ev, ev_0, mm):
    B = E["B"]
    NX, CH = 32, MB_CH
    t.wait_ge(a2.h, a2_h0 + hh + 1)
    if hh >= 2:
        t.wait_ge(ev.h, ev_0 + hh - 1)
    for sc in range(2):
        t.matmul(ps_yd[hh % 2][0:64, :],
                 B["x_tm"][:, sc * NX * 128 + (hh // 2) * 128 +
                           (hh % 2) * 64:
                           sc * NX * 128 + (hh // 2) * 128 + (hh % 2) * 64 + 64],
                 B["m2_sb"][:, (hh % 2) * 2 * CH + sc * CH:
                            (hh % 2) * 2 * CH + (sc + 1) * CH],
                 start=(sc == 0), stop=(sc == 1))
    ins = None
    for sc in range(2):
        ins = t.matmul(ps_s[hh % 2][:, 0:64],
                       B["bds_sb"][:, (hh % 2) * 256 + sc * 128:
                                   (hh % 2) * 256 + (sc + 1) * 128],
                       B["x_tm"][:, sc * NX * 128 + (hh // 2) * 128 +
                                 (hh % 2) * 64:
                                 sc * NX * 128 + (hh // 2) * 128 +
                                 (hh % 2) * 64 + 64],
                       start=(sc == 0), stop=(sc == 1))
    ins.then_inc(mm.h, 1)


def emit_mamba_pass2(K, scope, tg, E):
    ld, st, mm, ev, a1, a2, a3, cc = (
        K.sem[n] for n in ["ld", "st", "mm", "ev", "a1", "a2", "a3", "cc"])
    NC, NX, CH = MB_NC, 32, MB_CH
    B = E["B"]

    P2 = {}
    P2["s0"] = ssb(K, scope, "s0", [128, MB_H * MB_P], BF16)
    P2["prevl"] = ssb(K, scope, "prevl", [128, MB_H * MB_P], BF16)
    P2["ctile"] = ssb(K, scope, "ctile", [128, CH], BF16)
    P2["acs2"] = ssb(K, scope, "acs2", [64, CH], F32)
    P2["zar"] = ssb(K, scope, "zar", [128, NX * CH], BF16)
    P2["zsil"] = ssb(K, scope, "zsil", [128, NX * CH], BF16)
    P2["ypl"] = ssb(K, scope, "ypl", [128, NX * CH], BF16)
    P2["yga"] = ssb(K, scope, "yga", [128, NX * CH], F32)
    P2["sd_sb"] = ssb(K, scope, "sd_sb", [64, 2 * CH], BF16)
    P2["pf_sb"] = ssb(K, scope, "pf_sb", [128, 2 * 64], BF16)
    P2["tmp_sb"] = ssb(K, scope, "tmp_sb", [128, 2 * CH], F32)
    P2["dcb_sb"] = ssb(K, scope, "dcb_sb", [128, 64], F32)
    P2["dc_row"] = ssb(K, scope, "dc_row", [1, 64], BF16)
    P2["sqt"] = ssb(K, scope, "sqt", [128, 2 * CH], BF16)
    P2["rstd_w"] = ssb(K, scope, "rstd_w", [128, CH], F32)
    P2["ystage"] = ssb(K, scope, "ystage", [128, NX * CH], BF16)
    pscope = ExitStack()
    P2["ps_r"] = [sps(K, pscope, "p2_r", [128, CH]) for _ in range(2)]
    P2["ps_yo"] = [sps(K, pscope, "p2_yo", [128, CH]) for _ in range(2)]
    P2["ps_m"] = sps(K, pscope, "p2_m", [128, CH])

    ph_i = Phase(K, tg + "_p2init")
    ld0 = ld.v
    ld.reserve(16)
    a2_s0 = a2.reserve(1)

    @ph_i.on("sync")
    def _(sy):
        sy.wait_ge(cc.h, E["cc_state"])
        sy.wait_ge(st.h, E["p1_st_done"])
        sy.dma_start(out=P2["s0"][:, :], in_=E["ag_st"][0:128, :]
                     ).then_inc(ld.h, 16)

    @ph_i.on("vector")
    def _(v):
        v.wait_ge(ld.h, ld0 + 16)
        v.tensor_scalar(out=P2["s0"][:, :], in0=P2["s0"][:, :],
                        scalar1=E["kapc"][:, 0:1], scalar2=None,
                        op0=ALU.mult).then_inc(a2.h, 1)
    ph_i.emit()
    E["_p2_s0"] = a2_s0

    for c in range(NC):
        _mb_p2_chunk(K, E, P2, c)
    pscope.close()


def _mb_p2_chunk(K, E, P2, c):
    ld, st, mm, ev, a1, a2, a3 = (
        K.sem[n] for n in ["ld", "st", "mm", "ev", "a1", "a2", "a3"])
    NC, NX, CH = MB_NC, 32, MB_CH
    t0 = c * CH
    ph = Phase(K, E["tg"] + f"_p2c{c}")

    ld0 = ld.v
    ld_all = ld.reserve(16 * 11)
    ld_prev = ld_all
    ld_c = ld_all
    ld_acs = ld_all
    ld_z = [ld_all] * 4
    ld_yp = [ld_all] * 4
    st0 = st.v
    st.reserve(16 * 4)
    E["_p2_st_next"] = st.v

    @ph.on("sync")
    def _(sy):
        sy.dma_start(out=P2["prevl"][:, :],
                     in_=E["prev"][:, c * MB_H * MB_P:(c + 1) * MB_H * MB_P]
                     ).then_inc(ld.h, 16)
        sy.dma_start(out=P2["ctile"][:, :], in_=E["xc"][:, t0:t0 + CH]
                     ).then_inc(ld.h, 16)
        sy.dma_start(out=P2["acs2"][:, :], in_=E["acs"][:, t0:t0 + CH]
                     ).then_inc(ld.h, 16)
        for g in range(4):
            sy.dma_start(
                out=P2["zar"][:, :].rearrange("p (f l) -> p f l", f=NX)
                [:, g * 8:(g + 1) * 8, :],
                in_=E["zx"].rearrange("(f p) t -> p f t", p=128)
                [:, g * 8:(g + 1) * 8, t0:t0 + CH]).then_inc(ld.h, 16)
        for g in range(4):
            sy.dma_start(
                out=P2["ypl"][:, :].rearrange("p (f l) -> p f l", f=NX)
                [:, g * 8:(g + 1) * 8, :],
                in_=E["yp"].rearrange("(f p) t -> p f t", p=128)
                [:, g * 8:(g + 1) * 8, t0:t0 + CH]).then_inc(ld.h, 16)
        sy.wait_ge(a2.h, E["_p2_fin"])
        for g in range(4):
            sy.dma_start(
                out=E["yf"].rearrange("(f p) t -> p f t", p=128)
                [:, g * 8:(g + 1) * 8, t0:t0 + CH],
                in_=P2["ystage"][:, :].rearrange("p (f l) -> p f l", f=NX)
                [:, g * 8:(g + 1) * 8, :]).then_inc(st.h, 16)

    mm_0 = mm.v
    mm_dcum_tr = mm.reserve(1)
    mm_dcb = mm.reserve(1)
    plan_R, plan_YO = [], []
    cm = mm.v
    for h in range(MB_H):
        cm += 1
        plan_R.append(cm)
        cm += 1
        plan_YO.append(cm)
    mm.reserve(cm - mm.v)
    mm_rms0 = mm.v
    mm.reserve(NX)
    a1_0 = a1.v
    a1_dcum_row = a1.reserve(1)
    a1_dcb = a1.reserve(1)
    a1_h0 = a1.v
    a1.reserve(MB_H)
    a1_zs = a1.reserve(1)
    a1_sq0 = a1.v
    a1.reserve(NX)
    a1_rstd = a1.reserve(1)
    a2_0 = a2.v
    a2_h0 = a2.v
    a2.reserve(MB_H)
    a2_gate = a2.reserve(1)
    a2_rst1 = a2.reserve(1)
    a2_fin = a2.reserve(1)
    E["_p2_fin"] = a2_fin
    ev_0 = ev.v
    ev.reserve(MB_H)

    @ph.on("tensor")
    def _(t):
        t.wait_ge(a2.h, E["_p2_s0"])
        t.transpose(P2["ps_m"][0:1, 0:64], E["dcum_sb"][0:64, c:c + 1],
                    K.C.identf[0:64, 0:64]).then_inc(mm.h, 1)
        t.wait_ge(a1.h, a1_dcum_row)
        t.matmul(P2["ps_m"][:, 0:64], K.C.ones_row[0:1, 0:128],
                 P2["dc_row"][0:1, :], start=True, stop=True).then_inc(mm.h, 1)
        t.wait_ge(ld.h, ld_acs)
        t.wait_ge(ld.h, ld_c)
        for h in range(MB_H):
            if h >= 2:
                t.wait_ge(a1.h, a1_h0 + h - 1)
            t.matmul(P2["ps_r"][h % 2][:, :],
                     E["selb"][:, h * 128:(h + 1) * 128],
                     P2["acs2"][:, :], start=True, stop=True).then_inc(mm.h, 1)
            t.wait_ge(a2.h, a2_h0 + h + 1)
            if h >= 2:
                t.wait_ge(ev.h, ev_0 + h - 1)
            t.matmul(P2["ps_yo"][h % 2][0:64, :],
                     P2["pf_sb"][:, (h % 2) * 64:(h % 2 + 1) * 64],
                     P2["ctile"][:, :], start=True, stop=True).then_inc(mm.h, 1)
        for ft in range(NX):
            t.wait_ge(a1.h, a1_sq0 + ft + 1)
            t.matmul(P2["ps_m"][:, :], K.C.allones[:, :],
                     P2["sqt"][:, (ft % 2) * CH:(ft % 2 + 1) * CH],
                     start=(ft == 0), stop=(ft == NX - 1)).then_inc(mm.h, 1)

    @ph.on("scalar")
    def _(s):
        s.wait_ge(mm.h, mm_dcum_tr)
        s.activation(out=P2["dc_row"][0:1, :], in_=P2["ps_m"][0:1, 0:64],
                     func=AF.Copy).then_inc(a1.h, 1)
        s.wait_ge(mm.h, mm_dcb)
        s.activation(out=P2["dcb_sb"][:, :], in_=P2["ps_m"][:, 0:64],
                     func=AF.Copy).then_inc(a1.h, 1)
        for h in range(MB_H):
            s.wait_ge(mm.h, plan_R[h])
            s.activation(out=P2["sd_sb"][0:64, (h % 2) * CH:(h % 2 + 1) * CH],
                         in_=P2["ps_r"][h % 2][0:64, :],
                         func=AF.Exp).then_inc(a1.h, 1)
        for g in range(4):
            s.wait_ge(ld.h, ld_z[g])
        lastop = None
        for ft in range(NX):
            lastop = s.activation(out=P2["zsil"][:, ft * CH:(ft + 1) * CH],
                                  in_=P2["zar"][:, ft * CH:(ft + 1) * CH],
                                  func=AF.Silu)
        lastop.then_inc(a1.h, 1)
        s.wait_ge(a2.h, a2_gate)
        for ft in range(NX):
            if ft >= 2:
                s.wait_ge(mm.h, mm_rms0 + ft - 1)
            s.activation(out=P2["sqt"][:, (ft % 2) * CH:(ft % 2 + 1) * CH],
                         in_=P2["yga"][:, ft * CH:(ft + 1) * CH],
                         func=AF.Square).then_inc(a1.h, 1)
        s.wait_ge(a2.h, a2_rst1)
        s.activation(out=P2["rstd_w"][:, :], in_=P2["rstd_w"][:, :],
                     func=AF.Sqrt).then_inc(a1.h, 1)

    @ph.on("vector")
    def _(v):
        for h in range(MB_H):
            if h == 0:
                v.wait_ge(a1.h, a1_dcb)
                v.wait_ge(ld.h, ld_prev)
                for g in range(4):
                    v.wait_ge(ld.h, ld_yp[g])
            v.scalar_tensor_tensor(
                out=P2["pf_sb"][:, (h % 2) * 64:(h % 2 + 1) * 64],
                in0=P2["s0"][:, h * 64:(h + 1) * 64],
                scalar=P2["dcb_sb"][:, h:h + 1],
                in1=P2["prevl"][:, h * 64:(h + 1) * 64],
                op0=ALU.mult, op1=ALU.add).then_inc(a2.h, 1)
            v.wait_ge(mm.h, plan_YO[h])
            v.wait_ge(a1.h, a1_h0 + h + 1)
            base = (h % 2) * 64
            ft = h // 2
            v.tensor_tensor(out=P2["tmp_sb"][base:base + 64,
                                             (h % 2) * CH:(h % 2 + 1) * CH],
                            in0=P2["ps_yo"][h % 2][0:64, :],
                            in1=P2["sd_sb"][0:64, (h % 2) * CH:(h % 2 + 1) * CH],
                            op=ALU.mult)
            v.tensor_tensor(out=P2["yga"][base:base + 64,
                                          ft * CH:(ft + 1) * CH],
                            in0=P2["tmp_sb"][base:base + 64,
                                             (h % 2) * CH:(h % 2 + 1) * CH],
                            in1=P2["ypl"][base:base + 64,
                                          ft * CH:(ft + 1) * CH],
                            op=ALU.add).then_inc(ev.h, 1)
        v.wait_ge(a1.h, a1_zs)
        lastop = None
        for ft in range(NX):
            lastop = v.tensor_tensor(out=P2["yga"][:, ft * CH:(ft + 1) * CH],
                                     in0=P2["yga"][:, ft * CH:(ft + 1) * CH],
                                     in1=P2["zsil"][:, ft * CH:(ft + 1) * CH],
                                     op=ALU.mult)
        lastop.then_inc(a2.h, 1)
        v.wait_ge(mm.h, mm_rms0 + NX)
        K.bar("vector", v, v.tensor_scalar(
            out=P2["rstd_w"][:, :], in0=P2["ps_m"][:, :],
            scalar1=1.0 / MB_IN, scalar2=LN_EPS,
            op0=ALU.mult, op1=ALU.add))
        v.reciprocal(P2["rstd_w"][:, :], P2["rstd_w"][:, :]).then_inc(a2.h, 1)
        v.wait_ge(a1.h, a1_rstd)
        v.wait_ge(st.h, E.get("_p2_st_prev", 0))
        lastop = None
        for ft in range(NX):
            lastop = v.scalar_tensor_tensor(
                out=P2["ystage"][:, ft * CH:(ft + 1) * CH],
                in0=P2["yga"][:, ft * CH:(ft + 1) * CH],
                scalar=E["nwc"][:, ft:ft + 1],
                in1=P2["rstd_w"][:, :], op0=ALU.mult, op1=ALU.mult)
        lastop.then_inc(a2.h, 1)
    ph.emit()
    E["_p2_st_prev"] = E["_p2_st_next"]


def sub_mamba_factory(layer):
    def f(K, inp, x_res):
        return emit_mamba(K, inp, x_res, layer)
    return f


def emit_mamba(K, inp, x_res, layer):
    tg = f"mb{layer}"
    if layer == 0:
        w_in = pack_w(np.pad(np.asarray(inp["L0_attn_a_W_in"], np.float32),
                             ((0, 0), (0, 64))))
        convw = np.asarray(inp["L0_attn_a_conv_w"], np.float32).T
        convb = inp["L0_attn_a_conv_b"]
        dtb, alog, dvec = (inp["L0_attn_a_dt_bias"], inp["L0_attn_a_A_log"],
                           inp["L0_attn_a_D"])
        nwv = inp["L0_attn_a_norm_weight"]
        w_out = pack_w(inp["L0_attn_a_W_out"])
        clip100 = True
    else:
        w_in = pack_w(np.pad(np.asarray(inp["L1_attn_b_in_proj_kernel"],
                                        np.float32), ((0, 0), (0, 64))))
        convw = np.asarray(inp["L1_attn_b_conv1d_weight"], np.float32)
        convb = inp["L1_attn_b_conv1d_bias"]
        dtb, alog, dvec = (inp["L1_attn_b_dt_bias"], inp["L1_attn_b_A_log"],
                           inp["L1_attn_b_D"])
        nwv = inp["L1_attn_b_norm_weight"]
        w_out = pack_w(inp["L1_attn_b_out_proj_kernel"])
        clip100 = False

    w_in_ap = K.input(tg + "_win", w_in)
    w_out_ap = K.input(tg + "_wout", w_out)
    cw = np.ascontiguousarray(convw.reshape(34, 128, 4).transpose(1, 0, 2))
    cw_ap = K.input(tg + "_cw", cw.reshape(128, 34 * 4))
    cb_ap = K.input(tg + "_cb", pcol(convb))
    dtb_ap = K.input(tg + "_dtb", colv(dtb))
    alog_ap = K.input(tg + "_alog", colv(alog))
    dexp_ap = K.input(tg + "_dexp", pcol(np.repeat(np.asarray(dvec, np.float32), MB_P)))
    nw_ap = K.input(tg + "_nw", pcol(nwv))
    sel_ap = K.input("c_sel", const_host_arrays()["c_sel"])
    gm_ap = K.input("c_gmask", const_host_arrays()["c_gmask"])
    onesf_ap = K.input("c_onesf", np.ones((128, MB_CH), np.float32))
    kap_ap = K.input("kappa_col", [
        np.full((128, 1), 1.0 if (c % 2 == 1) else 0.0, np.float32)
        for c in range(NCORES)])

    zx_dram = K.scratch(tg + "_zx", [MB_DIP + 64, T], BF16)
    xc_dram = K.scratch(tg + "_xc", [128, T], BF16)
    acs_dram = K.scratch(tg + "_acs", [MB_H, T], F32)
    dtv_dram = K.scratch(tg + "_dtv", [MB_H, T], F32)
    prev_dram = K.scratch(tg + "_prev", [128, MB_NC * MB_H * MB_P], BF16)
    dcum_dram = K.scratch(tg + "_dcum", [MB_H, MB_NC], F32)
    st_dram = K.scratch(tg + "_st", [128, MB_H * MB_P], BF16)
    ag_st = K.scratch(tg + "_agst", [256, MB_H * MB_P], BF16)
    ypart_dram = K.scratch(tg + "_yp", [MB_IN, T], BF16)
    yfin_dram = K.scratch(tg + "_yf", [MB_IN, T], BF16)

    ld, wd, st, mm, ev, a1, a2, a3, cc, xr = (
        K.sem[n] for n in ["ld", "wd", "st", "mm", "ev", "a1", "a2", "a3",
                           "cc", "xr"])

    tail_dram = K.scratch(tg + "_tail", [128, 34 * 3], BF16)
    ag_tail = K.scratch(tg + "_agtail", [256, 34 * 3], BF16)

    with ExitStack() as scope:
        ln = emit_ln_xn(K, scope, x_res)
        stage = ssb(K, scope, "mstage", [128, 2 * T], BF16)
        zs = [K.dsem(tg + "_zx0"), K.dsem(tg + "_zx1")]
        zx_done_f = {}
        for f in range(67):
            zs[f % 2].reserve(16)
            if 32 <= f < 66:
                zs[f % 2].reserve(16)
            zx_done_f[f] = zs[f % 2].v
        st_zx_done = [(s, s.v) for s in zs]

        def evac1(ph, f, tt, ps, mm_val, ev_val):
            @ph.on("scalar")
            def _(s):
                s.wait_ge(mm.h, mm_val)
                if f >= 2 and tt == 0:
                    s.wait_ge(zs[f % 2].h, zx_done_f[f - 2])
                s.activation(out=stage[:, (f % 2) * T + tt * TT:
                                       (f % 2) * T + (tt + 1) * TT],
                             in_=ps[:, :], func=AF.Copy).then_inc(ev.h, 1)

            if tt == NTT - 1:
                @ph.on("sync")
                def _(sy):
                    sy.wait_ge(ev.h, ev_val)
                    sy.dma_start(out=zx_dram[f * 128:(f + 1) * 128, :],
                                 in_=stage[:, (f % 2) * T:(f % 2 + 1) * T]
                                 ).then_inc(zs[f % 2].h, 16)
                    if 32 <= f < 66:
                        sy.dma_start(out=tail_dram[:, (f - 32) * 3:
                                                   (f - 31) * 3],
                                     in_=stage[:, (f % 2) * T + T - 3:
                                               (f % 2) * T + T],
                                     allow_slow_non_contiguous=True
                                     ).then_inc(zs[f % 2].h, 16)

        with ExitStack() as ps1:
            emit_proj(K, ps1, tg + "_ip", w_dram=w_in_ap, fic=NFT, foc=67,
                      rhs_slice=lambda fi, tt: ln.xn[:, fi * T + tt * TT:
                                                     fi * T + (tt + 1) * TT],
                      ntt=NTT, tt_cols=TT, evac=evac1,
                      wait_sems=[(a2, ln.xn_done)])

    ph_ct = Phase(K, tg + "_ccq")
    cc0 = cc.v
    cc.reserve(1)

    @ph_ct.on("gpsimd")
    def _(g):
        for sem, fin in st_zx_done:
            g.wait_ge(sem.h, fin)
        g.collective_compute("AllGather", ALU.bypass,
                             replica_groups=[[0, 1], [2, 3], [4, 5], [6, 7]],
                             ins=[tail_dram[:, :]], outs=[ag_tail[:, :]]
                             ).then_inc(cc.h, 1)
        g.wait_ge(cc.h, cc0 + 1)
    ph_ct.emit()

    with ExitStack() as scope:
        selb = ssb(K, scope, "selb", [64, 64 * 128], F32)
        gmk = ssb(K, scope, "gmk", [128, 2 * MB_CH], F32)
        onesf = ssb(K, scope, "onesf", [128, MB_CH], F32)
        cwc = ssb(K, scope, "cwc", [128, 34 * 4], F32)
        cbc = ssb(K, scope, "cbc", [128, 34], F32)
        dtbc = ssb(K, scope, "dtbc", [64, 1], F32)
        acol = ssb(K, scope, "acol", [64, 1], F32)
        ac2 = ssb(K, scope, "ac2", [64, 1], F32)
        dxc = ssb(K, scope, "dxc", [128, 32], F32)
        nwc = ssb(K, scope, "nwc", [128, 32], F32)
        kapc = ssb(K, scope, "kapc", [128, 1], F32)
        dcum_sb = ssb(K, scope, "dcum", [64, MB_NC + 1], F32)
        ph0 = Phase(K, tg + "_consts")
        tgt = ld.reserve(16 * 10)
        a1_ac = a1.reserve(1)
        a2_ac = a2.reserve(1)

        @ph0.on("gpsimd")
        def _(g):
            g.dma_start(out=selb[:, :].rearrange("p (h q) -> p h q", h=64),
                        in_=sel_ap.rearrange("(h p) q -> p h q", p=64)
                        ).then_inc(ld.h, 16)
            g.dma_start(out=gmk[:, :].rearrange("p (j q) -> p j q", j=2),
                        in_=gm_ap.rearrange("(j p) q -> p j q", p=128)
                        ).then_inc(ld.h, 16)
            g.dma_start(out=onesf[:, :], in_=onesf_ap[:, :]).then_inc(ld.h, 16)
            g.dma_start(out=cwc[:, :], in_=cw_ap[:, :]).then_inc(ld.h, 16)
            g.dma_start(out=cbc[:, :], in_=cb_ap[:, :]).then_inc(ld.h, 16)
            g.dma_start(out=dtbc[:, :], in_=dtb_ap[:, :],
                        allow_slow_non_contiguous=True).then_inc(ld.h, 16)
            g.dma_start(out=ac2[:, :], in_=alog_ap[:, :],
                        allow_slow_non_contiguous=True).then_inc(ld.h, 16)
            g.dma_start(out=kapc[:, :], in_=kap_ap[:, :],
                        allow_slow_non_contiguous=True).then_inc(ld.h, 16)
            g.dma_start(out=dxc[:, :], in_=dexp_ap[:, :]).then_inc(ld.h, 16)
            g.dma_start(out=nwc[:, :], in_=nw_ap[:, :]).then_inc(ld.h, 16)
            g.wait_ge(ld.h, tgt)

        @ph0.on("scalar")
        def _(s):
            s.wait_ge(ld.h, tgt)
            s.activation(out=ac2[:, :], in_=ac2[:, :],
                         func=AF.Exp).then_inc(a1.h, 1)

        @ph0.on("vector")
        def _(v):
            v.wait_ge(a1.h, a1_ac)
            v.tensor_scalar_mul(acol[:, :], ac2[:, :], -1.0).then_inc(a2.h, 1)
        ph0.emit()

        E = dict(
            tg=tg, zx=zx_dram, xc=xc_dram, acs=acs_dram, dtv=dtv_dram,
            prev=prev_dram, stf=st_dram, yp=ypart_dram, yf=yfin_dram,
            ag_tail=ag_tail, ag_st=ag_st, selb=selb, gmk=gmk, onesf=onesf,
            cwc=cwc, cbc=cbc, dtbc=dtbc, acol=acol, dxc=dxc, nwc=nwc,
            kapc=kapc, dcum_sb=dcum_sb, clip100=clip100,
            consts_done=(a2, a2_ac), st_zx_done=st_zx_done)

        emit_mamba_pass1(K, scope, tg, E)

        ph_cc = Phase(K, tg + "_cc")
        cc1 = cc.v
        cc.reserve(1)

        @ph_cc.on("gpsimd")
        def _(g):
            g.collective_compute("AllGather", ALU.bypass,
                                 replica_groups=[[0, 1], [2, 3], [4, 5], [6, 7]],
                                 ins=[st_dram[:, :]], outs=[ag_st[:, :]]
                                 ).then_inc(cc.h, 1)
            g.wait_ge(cc.h, cc1 + 1)
        ph_cc.emit()
        E["cc_state"] = cc.v

        emit_mamba_pass2(K, scope, tg, E)
    st_p2_done = st.v

    TB = 1024
    for tb in range(2):
        with ExitStack() as s2:
            yarena = ssb(K, s2, "yarena", [128, 32 * TB], BF16)
            ph_ld = Phase(K, tg + f"_yld{tb}")
            ld0 = ld.v
            tgt = ld.reserve(16 * 32)

            @ph_ld.on("sync")
            def _(sy, tb=tb):
                if tb == 0:
                    sy.wait_ge(st.h, st_p2_done)
                for f in range(32):
                    sy.dma_start(out=yarena[:, f * TB:(f + 1) * TB],
                                 in_=yfin_dram[f * 128:(f + 1) * 128,
                                               tb * TB:(tb + 1) * TB]
                                 ).then_inc(ld.h, 16)
                sy.wait_ge(ld.h, tgt)
            ph_ld.emit()

            xwin = x_res[:, tb * TB:(tb + 1) * TB]
            evac2, fin2 = make_residual_evac(K, s2, xwin, NFT, TB // TT, TT)
            ph2 = emit_proj(K, s2, tg + f"_op{tb}", w_dram=w_out_ap,
                            fic=32, foc=NFT,
                            rhs_slice=lambda fi, tt: yarena[:, fi * TB + tt * TT:
                                                            fi * TB + (tt + 1) * TT],
                            ntt=TB // TT, tt_cols=TT, evac=evac2,
                            emit_now=False)
            fin2(ph2)
            ph2.emit()



def sub_not_impl(K, inp, x_res):
    raise NotImplementedError


SUBS = [
    ("mamba0", sub_mamba_factory(0)),
    ("ffn", sub_ffn),
    ("mhsa", sub_mhsa),
    ("ffn2", sub_ffn),
    ("gqa", sub_gqa),
    ("moe", sub_moe),
    ("mamba1", sub_mamba_factory(1)),
    ("moe2", sub_moe),
]


def shard_x(x):
    xs = []
    for c in range(NCORES):
        b, h = c // 2, c % 2
        xs.append(np.ascontiguousarray(
            np.asarray(x[b, h * T:(h + 1) * T, :], dtype=np.float32).T))
    return xs


def unshard_x(xs):
    out = np.empty((4, 2 * T, D), dtype=np.float32)
    for c in range(NCORES):
        b, h = c // 2, c % 2
        out[b, h * T:(h + 1) * T, :] = xs[c].T
    return out


def build_and_run(sub_ids, xs, inp):
    K = Ctx(f"g{sub_ids[0]}")
    x_in = K.input("x_in", xs)
    x_res = K.scratch("x_res", [D, T], F32)
    x_out = K.output("x_out", [D, T])
    ld = K.sem["ld"]

    ph = Phase(K, "xin")
    tgt = ld.reserve(16)

    @ph.on("sync")
    def _(sy):
        sy.dma_start(out=x_res[:, :], in_=x_in[:, :]).then_inc(ld.h, 16)
        sy.wait_ge(ld.h, tgt)
    ph.emit()

    K.C = emit_consts(K)

    for si in sub_ids:
        SUBS[si][1](K, inp, x_res)

    st = K.sem["st"]
    ph = Phase(K, "xout")
    n_dbg = len(K.dbg)
    tgt = st.reserve(16 * (1 + n_dbg))
    dbg_outs = [K.output("dbg_" + nm, shape, dtype)
                for (nm, ap, shape, dtype) in K.dbg]

    @ph.on("sync")
    def _(sy):
        sy.dma_start(out=x_out[:, :], in_=x_res[:, :]).then_inc(st.h, 16)
        for (nm, ap, shape, dtype), o in zip(K.dbg, dbg_outs):
            sy.dma_start(out=o[:, :], in_=ap[:, :]).then_inc(st.h, 16)
        sy.wait_ge(st.h, tgt)
    ph.emit()

    K.es.close()
    res = run_bass_kernel_spmd(K.nc, K.in_maps, core_ids=list(range(NCORES)))
    if K.dbg:
        global DEBUG_OUT
        DEBUG_OUT = [{nm: np.asarray(res.results[c]["dbg_" + nm])
                      for (nm, _, _, _) in K.dbg} for c in range(NCORES)]
    return [np.asarray(res.results[c]["x_out"], dtype=np.float32)
            for c in range(NCORES)]


def kernel(**inp):
    inp = {k: np.asarray(v) for k, v in inp.items()}
    xs = shard_x(inp["L0_attn_a_hidden_states"])
    sub_ids = list(range(SUB_FROM, SUB_TO))
    if FUSED:
        groups = [sub_ids]
    else:
        groups = [[i] for i in sub_ids]
    for grp in groups:
        xs = build_and_run(grp, xs, inp)
    return unshard_x(xs)
